# version 04
# baseline (speedup 1.0000x reference)
import math
import contextlib
import numpy as np
import ml_dtypes
import concourse.bass as bass
import concourse.mybir as mybir
from concourse.bass_utils import run_bass_kernel_spmd

F32 = mybir.dt.float32
BF16 = mybir.dt.bfloat16
I32 = mybir.dt.int32
U8 = mybir.dt.uint8
AF = mybir.ActivationFunctionType
ALU = mybir.AluOpType

T = 2048
D = 1024
DI = 2048
NSTATE = 16
DTRANK = 64
EPS = 1e-5
TCW = 512

EN_L0A = True
EN_L0F = True
EN_L1A = True
EN_L1F = True
DEBUG_FFN = False
DEBUG_GLA = False


def _gelu_consts():
    X = np.linspace(-4.0, 4.0, 33)
    g2 = 0.5 * X * np.vectorize(math.erf)(X / np.sqrt(2.0))
    Ey = g2[16:]
    m = (Ey[1:] - Ey[:-1]) / 0.25
    return float(np.float32(m[0])), [float(np.float32(v)) for v in (m[1:] - m[:-1])]


GELU_M0, GELU_DM = _gelu_consts()


def _float_pool_values():
    vals = [EPS, GELU_M0, 1.0 / 127.0, 1.0 / 16129.0, math.log(1.0 / 16.0)]
    for i in range(1, 16):
        ad = abs(GELU_DM[i - 1])
        vals.append(ad)
        vals.append(-ad * 64.0 * i)
    out = []
    for v in vals:
        out.append(float(np.float32(v)))
    return sorted(set(out))


def _consts_inputs():
    identf = np.eye(128, dtype=np.float32)
    identb = np.eye(128, dtype=ml_dtypes.bfloat16)
    mask_ut = np.tile(np.triu(np.ones((64, 64), dtype=np.float32)), (2, 1))
    pool = np.tile(np.array(_float_pool_values(), dtype=np.float32).reshape(1, -1),
                   (128, 1))
    return {"c_identf": identf, "c_identb": identb, "c_maskut": mask_ut,
            "c_pool": pool}


def FC(v):
    return float(np.float32(v))


class Ctr:
    def __init__(self, es, nc, name):
        self.h = es.enter_context(nc.semaphore(name))
        self.n = 0


class P:
    pass


def load_cols(p):
    nc = p.nc
    vecs = [("L0_attn_ln_g", D), ("L0_attn_ln_b", D),
            ("L0_attn_f_conv_b", DI), ("L0_attn_b_conv_b", DI),
            ("L0_attn_f_dt_b", DI), ("L0_attn_b_dt_b", DI),
            ("L0_attn_f_D_skip", DI), ("L0_attn_b_D_skip", DI),
            ("L0_ffn_b1", 4096), ("L0_ffn_b2", D),
            ("L1_attn_bgk2", 1024), ("L1_attn_rms_w", 512)]
    sb = nc.alloc_sbuf_tensor
    p.cols = {}
    for name, ln in vecs:
        p.cols[name] = sb("col_" + name, [128, ln // 128], F32)
    p.Acol = {"f": sb("Acol_f", [128, NSTATE * 16], F32),
              "b": sb("Acol_b", [128, NSTATE * 16], F32)}
    p.convw = {"f": sb("convw_f", [128, 64], F32),
               "b": sb("convw_b", [128, 64], F32)}
    p.b1c256 = sb("b1c256", [128, 32], F32)
    p.b2c256 = sb("b2c256", [128, 8], F32)

    with nc.sbuf_tensor("vec_stage", [128, 128], F32) as stage:
        for name, ln in vecs:
            a = ln // 128
            with nc.Block() as blk:
                @blk.sync
                def _(sync):
                    sync.dma_start(stage[0:a, 0:128],
                                   p.win[name].rearrange("(a b) -> a b", b=128))\
                        .then_inc(p.qS.h, 16)
                    p.qS.n += 16
                    sync.wait_ge(p.qS.h, p.qS.n)
            with nc.Block() as blk:
                @blk.tensor
                def _(t):
                    t.transpose(p.pst[0][:, 0:128], stage[:, 0:128], p.identf_t[:])\
                        .then_inc(p.tS.h, 1)
                    p.tS.n += 1

                @blk.scalar
                def _(s):
                    s.wait_ge(p.tS.h, p.tS.n)
                    s.copy(p.cols[name][:, 0:a], p.pst[0][:, 0:a]).then_inc(p.sS.h, 1)
                    p.sS.n += 1

    with nc.Block() as blk:
        @blk.sync
        def _(sync):
            for d, key in [("f", "L0_attn_f_A_log"), ("b", "L0_attn_b_A_log")]:
                for ch in range(16):
                    sync.dma_start(p.Acol[d][:, ch * 16:(ch + 1) * 16],
                                   p.win[key][ch * 128:(ch + 1) * 128, :])\
                        .then_inc(p.qS.h, 16)
                    p.qS.n += 16
            for d, key in [("f", "L0_attn_f_conv_w"), ("b", "L0_attn_b_conv_w")]:
                for ch in range(16):
                    sync.dma_start(p.convw[d][:, ch * 4:(ch + 1) * 4],
                                   p.win[key][ch * 128:(ch + 1) * 128, :])\
                        .then_inc(p.qS.h, 16)
                    p.qS.n += 16
            sync.wait_ge(p.qS.h, p.qS.n)

    with nc.Block() as blk:
        @blk.scalar
        def _(s):
            for d in ["f", "b"]:
                s.activation(p.Acol[d][:], p.Acol[d][:], AF.Exp)
            s.mul(p.b1c256[:], p.cols["L0_ffn_b1"][:], 256.0)
            s.mul(p.b2c256[:], p.cols["L0_ffn_b2"][:], 256.0)

    with nc.Block() as blk:
        @blk.vector
        def _(v):
            for d in ["f", "b"]:
                v.tensor_scalar(p.Acol[d][:], p.Acol[d][:], -1.0, None, ALU.mult)


def prep_weights_gpsimd(p, g, wbounce):
    names = p.prep_names
    jobs = []
    for name in names:
        R, C = p.win[name].shape
        for r0 in range(0, R, 128):
            jobs.append((name, r0, min(r0 + 128, R)))
    qg0 = p.qG.n
    qh0 = p.qH.n
    for i, (name, r0, r1) in enumerate(jobs):
        C = p.win[name].shape[1]
        buf = wbounce[0:r1 - r0, (i % 2) * 4096:(i % 2) * 4096 + C]
        if i >= 2:
            g.wait_ge(p.qH.h, qh0 + 16 * (i - 1))
        g.dma_start(buf, p.win[name][r0:r1, :]).then_inc(p.qG.h, 16)
        p.qG.n += 16
        g.wait_ge(p.qG.h, p.qG.n)
        g.dma_start(p.wscr[name][r0:r1, :], buf).then_inc(p.qH.h, 16)
        p.qH.n += 16
    g.wait_ge(p.qH.h, p.qH.n)


def prep_weights(p):
    nc = p.nc
    names = ["L0_attn_f_in_w", "L0_attn_b_in_w", "L0_attn_f_out_w", "L0_attn_b_out_w",
             "L0_attn_f_xproj_w", "L0_attn_b_xproj_w", "L0_attn_f_dt_w", "L0_attn_b_dt_w",
             "L0_ffn_w1", "L0_ffn_w2",
             "L1_attn_Wq", "L1_attn_Wk", "L1_attn_Wv", "L1_attn_Wg",
             "L1_attn_Wgk1", "L1_attn_Wgk2", "L1_attn_Wo"]
    p.wscr = {}
    for name in names:
        R, C = p.win[name].shape
        p.wscr[name] = nc.dram_tensor("scr_" + name, [R, C], BF16, kind="Internal").ap()

    p.prep_names = names
    p.wg_t = nc.dram_tensor("scr3_gate", [24 * 128, 1024], BF16, kind="Internal").ap()
    p.wu_t = nc.dram_tensor("scr3_up", [24 * 128, 1024], BF16, kind="Internal").ap()
    p.wd_t = nc.dram_tensor("scr3_down", [24 * 128, 1024], BF16, kind="Internal").ap()


def input_transpose(p):
    nc = p.nc
    njobs = ([("g", s) for s in range(24)] + [("u", s) for s in range(24)]
             + [("d", m) for m in range(8)])
    ntr = [24 if k == "d" else 8 for k, _ in njobs]
    cumT = []
    acc = 0
    for n in ntr:
        acc += n
        cumT.append(acc)
    NJ = len(njobs)
    with nc.sbuf_tensor("xtok_in", [128, 16 * D], F32) as xtok, \
         nc.sbuf_tensor("wbounce", [128, 2 * 4096], BF16) as wbounce, \
         nc.sbuf_tensor("obuf", [128, 2 * 1024], BF16) as obuf, \
         nc.sbuf_tensor("obufd", [128, 4 * 128], BF16) as obufd:
        with nc.Block() as blk:
            q0 = p.qS.n
            qg0 = p.qG.n
            qx0 = [p.qX[i].n for i in range(2)]
            qw0 = [p.qW[i].n for i in range(4)]
            t_base = p.tS.n
            s_base = p.sS.n
            tiles = [(tt, dc) for tt in range(T // 128) for dc in range(8)]
            NX = len(tiles)

            @blk.sync
            def _(sync):
                for tt in range(T // 128):
                    sync.dma_start(xtok[:, tt * D:(tt + 1) * D],
                                   p.xin[tt * 128:(tt + 1) * 128, :]).then_inc(p.qS.h, 16)
                    p.qS.n += 16

            @blk.gpsimd
            def _(g):
                srcs = {"g": p.win["L1_ffn_w_gate"], "u": p.win["L1_ffn_w_up"],
                        "d": p.win["L1_ffn_w_down"]}
                for j, (kind, idx) in enumerate(njobs):
                    ncols = 3072 if kind == "d" else 1024
                    buf = wbounce[0:128, (j % 2) * 4096:(j % 2) * 4096 + ncols]
                    if j >= 2:
                        g.wait_ge(p.tS.h, t_base + NX + cumT[j - 2])
                    g.dma_start(buf, srcs[kind][idx * 128:(idx + 1) * 128, :])\
                        .then_inc(p.qG.h, 16)
                    p.qG.n += 16
                g.wait_ge(p.tS.h, t_base + NX + cumT[NJ - 1])
                prep_weights_gpsimd(p, g, wbounce)

            @blk.tensor
            def _(t):
                t.wait_ge(p.qS.h, q0 + 16 * (T // 128))
                for i, (tt, dc) in enumerate(tiles):
                    if i >= 2:
                        t.wait_ge(p.sS.h, s_base + i - 1)
                    t.transpose(p.pst[i % 2][:, 0:128],
                                xtok[:, tt * D + dc * 128: tt * D + dc * 128 + 128],
                                p.identf_t[:]).then_inc(p.tS.h, 1)
                    p.tS.n += 1
                gi = NX
                for j, (kind, idx) in enumerate(njobs):
                    t.wait_ge(p.qG.h, qg0 + 16 * (j + 1))
                    buf = wbounce[0:128, (j % 2) * 4096:]
                    for i in range(ntr[j]):
                        if gi >= 2:
                            t.wait_ge(p.sS.h, s_base + gi - 1)
                        t.transpose(p.pst[gi % 2].bitcast(BF16)[:, 0:128],
                                    buf[:, i * 128:(i + 1) * 128],
                                    p.identb_t[:]).then_inc(p.tS.h, 1)
                        p.tS.n += 1
                        gi += 1

            @blk.scalar
            def _(s):
                for i, (tt, dc) in enumerate(tiles):
                    s.wait_ge(p.tS.h, t_base + i + 1)
                    s.copy(p.r[:, dc * T + tt * 128: dc * T + tt * 128 + 128],
                           p.pst[i % 2][:, 0:128]).then_inc(p.sS.h, 1)
                    p.sS.n += 1
                gi = NX
                ti = 0
                for j, (kind, idx) in enumerate(njobs):
                    if kind in ("g", "u"):
                        dstD = p.wg_t if kind == "g" else p.wu_t
                        if j >= 2:
                            s.wait_ge(p.qX[j % 2].h, qx0[j % 2] + 16 * (j // 2))
                        oslot = obuf[:, (j % 2) * 1024:(j % 2) * 1024 + 1024]
                        for i in range(8):
                            s.wait_ge(p.tS.h, t_base + gi + 1)
                            s.copy(oslot[:, i * 128:(i + 1) * 128],
                                   p.pst[gi % 2].bitcast(BF16)[:, 0:128])\
                                .then_inc(p.sS.h, 1)
                            p.sS.n += 1
                            gi += 1
                        s.dma_start(dstD[idx * 128:(idx + 1) * 128, :], oslot)\
                            .then_inc(p.qX[j % 2].h, 16)
                        p.qX[j % 2].n += 16
                    else:
                        for i in range(24):
                            s.wait_ge(p.tS.h, t_base + gi + 1)
                            if ti >= 4:
                                s.wait_ge(p.qW[ti % 4].h, qw0[ti % 4] + 16 * (ti // 4))
                            dslot = obufd[:, (ti % 4) * 128:(ti % 4) * 128 + 128]
                            s.copy(dslot, p.pst[gi % 2].bitcast(BF16)[:, 0:128])\
                                .then_inc(p.sS.h, 1)
                            p.sS.n += 1
                            s.dma_start(p.wd_t[i * 128:(i + 1) * 128,
                                               idx * 128:(idx + 1) * 128], dslot)\
                                .then_inc(p.qW[ti % 4].h, 16)
                            p.qW[ti % 4].n += 16
                            gi += 1
                            ti += 1
                for k in range(2):
                    s.wait_ge(p.qX[k].h, p.qX[k].n)
                for k in range(4):
                    s.wait_ge(p.qW[k].h, p.qW[k].n)


def output_transpose(p):
    nc = p.nc
    with nc.sbuf_tensor("xtok_out", [128, 16 * D], F32) as xtok:
        with nc.Block() as blk:
            tiles = [(tt, dc) for tt in range(T // 128) for dc in range(8)]
            t_base = p.tS.n
            s_base = p.sS.n

            @blk.tensor
            def _(t):
                for i, (tt, dc) in enumerate(tiles):
                    if i >= 2:
                        t.wait_ge(p.sS.h, s_base + i - 1)
                    t.transpose(p.pst[i % 2][:, 0:128],
                                p.r[:, dc * T + tt * 128: dc * T + tt * 128 + 128],
                                p.identf_t[:]).then_inc(p.tS.h, 1)
                    p.tS.n += 1

            @blk.scalar
            def _(s):
                for i, (tt, dc) in enumerate(tiles):
                    s.wait_ge(p.tS.h, t_base + i + 1)
                    s.copy(xtok[:, tt * D + dc * 128: tt * D + dc * 128 + 128],
                           p.pst[i % 2][:, 0:128]).then_inc(p.sS.h, 1)
                    p.sS.n += 1

        with nc.Block() as blk:
            @blk.sync
            def _(sync):
                sync.wait_ge(p.sS.h, p.sS.n)
                for tt in range(T // 128):
                    sync.dma_start(p.yout[tt * 128:(tt + 1) * 128, :],
                                   xtok[:, tt * D:(tt + 1) * D]).then_inc(p.qS.h, 16)
                    p.qS.n += 16
                sync.wait_ge(p.qS.h, p.qS.n)


def ln_stats_tc(p, blk, tile_of_dc, tcw, nchunk=8):
    inv_n = 1.0 / (nchunk * 128)
    t_base = p.tS.n
    s_base = p.sS.n
    v_base = p.vS.n

    @blk.scalar
    def _(s):
        for dc in range(nchunk):
            if dc >= 2:
                s.wait_ge(p.tS.h, t_base + 2 * dc - 2)
            sq = p.sqtile if dc % 2 == 0 else p.vtmp
            s.activation(sq[:, 0:tcw], tile_of_dc(dc), AF.Square).then_inc(p.sS.h, 1)
            p.sS.n += 1

    @blk.tensor
    def _(t):
        for dc in range(nchunk):
            t.matmul(p.psrow[0][0:1, 0:tcw], p.ones_col[:, 0:1], tile_of_dc(dc),
                     start=(dc == 0), stop=(dc == nchunk - 1)).then_inc(p.tS.h, 1)
            p.tS.n += 1
            t.wait_ge(p.sS.h, s_base + dc + 1)
            sq = p.sqtile if dc % 2 == 0 else p.vtmp
            t.matmul(p.psrow[1][0:1, 0:tcw], p.ones_col[:, 0:1],
                     sq[:, 0:tcw],
                     start=(dc == 0), stop=(dc == nchunk - 1)).then_inc(p.tS.h, 1)
            p.tS.n += 1

    @blk.vector
    def _(v):
        v.wait_ge(p.tS.h, t_base + 2 * nchunk)
        sl = slice(0, tcw)
        v.tensor_scalar(p.row_m[0:1, sl], p.psrow[0][0:1, 0:tcw], inv_n, None, ALU.mult)
        v.tensor_scalar(p.row_a[0:1, sl], p.psrow[1][0:1, 0:tcw], inv_n, None, ALU.mult)
        v.tensor_mul(p.row_b[0:1, sl], p.row_m[0:1, sl], p.row_m[0:1, sl])
        v.tensor_sub(p.row_v[0:1, sl], p.row_a[0:1, sl], p.row_b[0:1, sl])\
            .then_inc(p.vS.h, 1)
        p.vS.n += 1

    @blk.scalar
    def _(s):
        s.wait_ge(p.vS.h, v_base + 1)
        sl = slice(0, tcw)
        s.activation(p.row_a[0:1, sl], p.row_v[0:1, sl], AF.Sqrt, bias=FC(EPS))\
            .then_inc(p.sS.h, 1)
        p.sS.n += 1

    @blk.vector
    def _(v):
        v.wait_ge(p.sS.h, p.sS.n)
        sl = slice(0, tcw)
        v.reciprocal(p.row_rs[0:1, sl], p.row_a[0:1, sl]).then_inc(p.vS.h, 1)
        p.vS.n += 1


def bcast_row(p, t, row_ap, bank, tcw):
    t.matmul(p.psbc[bank][:, 0:tcw], p.ones_row[0:1, :], row_ap,
             start=True, stop=True).then_inc(p.tS.h, 1)
    p.tS.n += 1


def plain_preln_plane(p, dst_bf16):
    nc = p.nc
    for tc in range(T // TCW):
        with nc.Block() as blk:
            ln_stats_tc(p, blk,
                        lambda dc: p.r[:, dc * T + tc * TCW: dc * T + tc * TCW + TCW],
                        TCW, 8)
        with nc.Block() as blk:
            t_base = p.tS.n

            @blk.tensor
            def _(t):
                bcast_row(p, t, p.row_m[0:1, 0:TCW], 0, TCW)
                bcast_row(p, t, p.row_rs[0:1, 0:TCW], 1, TCW)

            @blk.vector
            def _(v):
                v.wait_ge(p.tS.h, t_base + 2)
                for dc in range(8):
                    sl = slice(dc * T + tc * TCW, dc * T + tc * TCW + TCW)
                    v.tensor_sub(p.vtmp[:, 0:TCW], p.r[:, sl], p.psbc[0][:, 0:TCW])
                    v.tensor_mul(dst_bf16[:, sl], p.vtmp[:, 0:TCW], p.psbc[1][:, 0:TCW])\
                        .then_inc(p.vS.h, 1)
                    p.vS.n += 1


def post_ln_residual_tc(p, tc, tile_of_dc, tcw=TCW):
    nc = p.nc
    with nc.Block() as blk:
        ln_stats_tc(p, blk, tile_of_dc, tcw, 8)
    with nc.Block() as blk:
        t_base = p.tS.n

        @blk.tensor
        def _(t):
            bcast_row(p, t, p.row_m[0:1, 0:tcw], 0, tcw)
            bcast_row(p, t, p.row_rs[0:1, 0:tcw], 1, tcw)

        @blk.vector
        def _(v):
            v.wait_ge(p.tS.h, t_base + 2)
            for dc in range(8):
                sl = slice(dc * T + tc * tcw, dc * T + tc * tcw + tcw)
                v.tensor_sub(p.vtmp[:, 0:tcw], tile_of_dc(dc), p.psbc[0][:, 0:tcw])
                v.tensor_mul(p.vtmp[:, 0:tcw], p.vtmp[:, 0:tcw], p.psbc[1][:, 0:tcw])
                v.tensor_add(p.r[:, sl], p.r[:, sl], p.vtmp[:, 0:tcw])\
                    .then_inc(p.vS.h, 1)
                p.vS.n += 1


def build_l0_ffn(p):
    nc = p.nc
    if DEBUG_FFN:
        p.dbg_xn = nc.dram_tensor("dbg_xn", [128, 8 * T], BF16, kind="ExternalOutput").ap()
        p.dbg_hq = nc.dram_tensor("dbg_hq", [128, 32 * TCW], BF16, kind="ExternalOutput").ap()
        p.dbg_op = nc.dram_tensor("dbg_op", [128, 8 * TCW], F32, kind="ExternalOutput").ap()
    with contextlib.ExitStack() as es:
        xn = es.enter_context(nc.sbuf_tensor("ffn_xn", [128, 8 * T], BF16))
        hq = es.enter_context(nc.sbuf_tensor("ffn_hq", [128, 32 * TCW], BF16))
        wt = es.enter_context(nc.sbuf_tensor("ffn_wt", [128, 4 * 128], BF16))
        gk = es.enter_context(nc.sbuf_tensor("ffn_gk", [128, 4 * TCW], I32))
        ga = es.enter_context(nc.sbuf_tensor("ffn_ga", [128, TCW], F32))
        gr = es.enter_context(nc.sbuf_tensor("ffn_gr", [128, 2 * TCW], F32))
        gm = es.enter_context(nc.sbuf_tensor("ffn_gm", [128, TCW], U8))
        op = es.enter_context(nc.sbuf_tensor("ffn_op", [128, 8 * TCW], F32))

        plain_preln_plane(p, xn)

        w1s = p.wscr["L0_ffn_w1"]
        w2s = p.wscr["L0_ffn_w2"]
        for tc in range(T // TCW):
            with nc.Block() as blk:
                qw_base = [p.qW[i].n for i in range(4)]
                t_base = p.tS.n
                s_base = p.sS.n
                v_base = p.vS.n

                @blk.sync
                def _(sync):
                    for fs in range(32):
                        for kc in range(8):
                            j = fs * 8 + kc
                            slot = j % 4
                            if j >= 4:
                                sync.wait_ge(p.tS.h, t_base + (j - 3))
                            sync.dma_start(wt[:, slot * 128:(slot + 1) * 128],
                                           w1s[kc * 128:(kc + 1) * 128,
                                               fs * 128:(fs + 1) * 128])\
                                .then_inc(p.qW[slot].h, 16)
                            p.qW[slot].n += 16

                @blk.tensor
                def _(t):
                    for fs in range(32):
                        if fs >= 2:
                            t.wait_ge(p.sS.h, s_base + 17 * (fs - 2) + 2)
                        for kc in range(8):
                            j = fs * 8 + kc
                            slot = j % 4
                            t.wait_ge(p.qW[slot].h, qw_base[slot] + 16 * (j // 4 + 1))
                            t.matmul(p.psA[fs % 2][:, 0:TCW],
                                     wt[:, slot * 128:(slot + 1) * 128],
                                     xn[:, kc * T + tc * TCW: kc * T + tc * TCW + TCW],
                                     start=(kc == 0), stop=(kc == 7)).then_inc(p.tS.h, 1)
                            p.tS.n += 1

                @blk.scalar
                def _(s):
                    for fs in range(32):
                        gslot = (fs % 2) * 2 * TCW
                        if fs >= 1:
                            s.wait_ge(p.vS.h, v_base + 20 * fs)
                        s.wait_ge(p.tS.h, t_base + 8 * (fs + 1))
                        s.activation(gk[:, gslot:gslot + TCW], p.psA[fs % 2][:, 0:TCW],
                                     AF.Identity, bias=p.b1c256[:, fs:fs + 1], scale=256.0)\
                            .then_inc(p.sS.h, 1)
                        s.activation(gk[:, gslot + TCW:gslot + 2 * TCW],
                                     p.psA[fs % 2][:, 0:TCW],
                                     AF.Abs, bias=p.b1c256[:, fs:fs + 1], scale=256.0)\
                            .then_inc(p.sS.h, 1)
                        p.sS.n += 2
                        for i in range(1, 16):
                            ad = abs(GELU_DM[i - 1])
                            if i >= 3:
                                s.wait_ge(p.vS.h, v_base + 20 * fs + (i - 2) + 2)
                            s.activation(gr[:, (i % 2) * TCW:(i % 2) * TCW + TCW],
                                         gk[:, gslot + TCW:gslot + 2 * TCW], AF.Relu,
                                         bias=FC(-ad * 64.0 * i), scale=FC(ad))\
                                .then_inc(p.sS.h, 1)
                            p.sS.n += 1

                @blk.vector
                def _(v):
                    for fs in range(32):
                        gslot = (fs % 2) * 2 * TCW
                        base_s = s_base + fs * 17
                        v.wait_ge(p.sS.h, base_s + 2)
                        v.tensor_scalar(ga[:, 0:TCW], gk[:, gslot:gslot + TCW],
                                        0.5, None, ALU.mult).then_inc(p.vS.h, 1)
                        v.scalar_tensor_tensor(ga[:, 0:TCW],
                                               gk[:, gslot + TCW:gslot + 2 * TCW],
                                               GELU_M0, ga[:, 0:TCW],
                                               ALU.mult, ALU.add).then_inc(p.vS.h, 1)
                        p.vS.n += 2
                        for i in range(1, 16):
                            sgn = 1.0 if GELU_DM[i - 1] >= 0 else -1.0
                            v.wait_ge(p.sS.h, base_s + 2 + i)
                            v.scalar_tensor_tensor(
                                ga[:, 0:TCW],
                                gr[:, (i % 2) * TCW:(i % 2) * TCW + TCW],
                                sgn, ga[:, 0:TCW], ALU.mult, ALU.add)\
                                .then_inc(p.vS.h, 1)
                            p.vS.n += 1
                        v.tensor_single_scalar(gm[:, 0:TCW], gk[:, gslot:gslot + TCW],
                                               1024.0, ALU.is_gt)
                        v.copy_predicated(ga[:, 0:TCW], gm[:, 0:TCW],
                                          gk[:, gslot:gslot + TCW])
                        v.tensor_single_scalar(gm[:, 0:TCW], gk[:, gslot:gslot + TCW],
                                               -1024.0, ALU.is_lt)
                        v.copy_predicated(ga[:, 0:TCW], gm[:, 0:TCW], p.zeros[:, 0:TCW])
                        v.tensor_scalar(gk[:, gslot:gslot + TCW], ga[:, 0:TCW],
                                        1.0, None, ALU.mult)
                        v.tensor_scalar(hq[:, fs * TCW:(fs + 1) * TCW],
                                        gk[:, gslot:gslot + TCW],
                                        1.0 / 256.0, None, ALU.mult).then_inc(p.vS.h, 3)
                        p.vS.n += 3

            with nc.Block() as blk:
                qw_base = [p.qW[i].n for i in range(4)]
                t_base = p.tS.n
                s_base = p.sS.n
                v_base = p.vS.n

                @blk.sync
                def _(sync):
                    for ds in range(8):
                        for kc in range(32):
                            j = ds * 32 + kc
                            slot = j % 4
                            if j >= 4:
                                sync.wait_ge(p.tS.h, t_base + (j - 3))
                            sync.dma_start(wt[:, slot * 128:(slot + 1) * 128],
                                           w2s[kc * 128:(kc + 1) * 128,
                                               ds * 128:(ds + 1) * 128])\
                                .then_inc(p.qW[slot].h, 16)
                            p.qW[slot].n += 16

                @blk.tensor
                def _(t):
                    for ds in range(8):
                        if ds >= 2:
                            t.wait_ge(p.sS.h, s_base + (ds - 1))
                        for kc in range(32):
                            j = ds * 32 + kc
                            slot = j % 4
                            t.wait_ge(p.qW[slot].h, qw_base[slot] + 16 * (j // 4 + 1))
                            t.matmul(p.psA[ds % 2][:, 0:TCW],
                                     wt[:, slot * 128:(slot + 1) * 128],
                                     hq[:, kc * TCW:(kc + 1) * TCW],
                                     start=(kc == 0), stop=(kc == 31)).then_inc(p.tS.h, 1)
                            p.tS.n += 1

                @blk.scalar
                def _(s):
                    for ds in range(8):
                        if ds >= 2:
                            s.wait_ge(p.vS.h, v_base + ds - 1)
                        s.wait_ge(p.tS.h, t_base + 32 * (ds + 1))
                        s.activation(gk[:, (ds % 2) * TCW:(ds % 2) * TCW + TCW],
                                     p.psA[ds % 2][:, 0:TCW], AF.Identity,
                                     bias=p.b2c256[:, ds:ds + 1], scale=256.0)\
                            .then_inc(p.sS.h, 1)
                        p.sS.n += 1

                @blk.vector
                def _(v):
                    for ds in range(8):
                        v.wait_ge(p.sS.h, s_base + ds + 1)
                        v.tensor_scalar(op[:, ds * TCW:(ds + 1) * TCW],
                                        gk[:, (ds % 2) * TCW:(ds % 2) * TCW + TCW],
                                        1.0 / 256.0, None, ALU.mult).then_inc(p.vS.h, 1)
                        p.vS.n += 1

            if DEBUG_FFN and tc == 0:
                with nc.Block() as blk:
                    @blk.sync
                    def _(sync):
                        sync.dma_start(p.dbg_xn, xn[:]).then_inc(p.qS.h, 16)
                        sync.dma_start(p.dbg_hq, hq[:]).then_inc(p.qS.h, 16)
                        sync.dma_start(p.dbg_op, op[:]).then_inc(p.qS.h, 16)
                        p.qS.n += 48
                        sync.wait_ge(p.qS.h, p.qS.n)
            post_ln_residual_tc(p, tc, lambda dc: op[:, dc * TCW:(dc + 1) * TCW])


def build_l0_attn(p):
    nc = p.nc
    NTC = 4
    W = 512

    scr_xc = {d: nc.dram_tensor("scr_xc_" + d, [DI, T], F32, kind="Internal").ap()
              for d in "fb"}
    scr_sz = {d: nc.dram_tensor("scr_sz_" + d, [DI, T], F32, kind="Internal").ap()
              for d in "fb"}
    scr_y2 = {d: nc.dram_tensor("scr_y2_" + d, [DI, T], BF16, kind="Internal").ap()
              for d in "fb"}
    scr_yf = nc.dram_tensor("scr_yf", [D, T], F32, kind="Internal").ap()

    with contextlib.ExitStack() as es0:
        dbltmp = {d: es0.enter_context(nc.sbuf_tensor("dbltmp_" + d, [96, T], F32))
                  for d in "fb"}
        dtvec = {d: es0.enter_context(nc.sbuf_tensor("dtvec_" + d, [64, T], BF16))
                 for d in "fb"}

        state = es0.enter_context(nc.sbuf_tensor("mb_state", [128, 256], F32))

        with contextlib.ExitStack() as es1:
            xn = es1.enter_context(nc.sbuf_tensor("mb_xn", [128, 8 * T], BF16))
            gcol = p.cols["L0_attn_ln_g"]
            bcol = p.cols["L0_attn_ln_b"]
            for tc in range(NTC):
                with nc.Block() as blk:
                    ln_stats_tc(p, blk,
                                lambda dc: p.r[:, dc * T + tc * W: dc * T + tc * W + W],
                                W, 8)
                with nc.Block() as blk:
                    v_base = p.vS.n
                    s_base = p.sS.n

                    @blk.vector
                    def _(v):
                        v.tensor_scalar(p.row_a[0:1, 0:W], p.row_v[0:1, 0:W], FC(EPS),
                                        None, ALU.add)
                        v.reciprocal(p.row_b[0:1, 0:W], p.row_a[0:1, 0:W])
                        v.tensor_mul(p.row_a[0:1, 0:W], p.row_v[0:1, 0:W],
                                     p.row_b[0:1, 0:W]).then_inc(p.vS.h, 1)
                        p.vS.n += 1

                    @blk.scalar
                    def _(s):
                        s.wait_ge(p.vS.h, v_base + 1)
                        s.activation(p.row_b[0:1, 0:W], p.row_a[0:1, 0:W], AF.Sqrt,
                                     bias=FC(EPS)).then_inc(p.sS.h, 1)
                        p.sS.n += 1

                    @blk.vector
                    def _(v):
                        v.wait_ge(p.sS.h, s_base + 1)
                        v.reciprocal(p.row_a[0:1, 0:W], p.row_b[0:1, 0:W])
                        v.tensor_mul(p.row_rs[0:1, 0:W], p.row_rs[0:1, 0:W],
                                     p.row_a[0:1, 0:W]).then_inc(p.vS.h, 1)
                        p.vS.n += 1

                with nc.Block() as blk:
                    t_base = p.tS.n

                    @blk.tensor
                    def _(t):
                        bcast_row(p, t, p.row_m[0:1, 0:W], 0, W)
                        bcast_row(p, t, p.row_rs[0:1, 0:W], 1, W)

                    @blk.vector
                    def _(v):
                        v.wait_ge(p.tS.h, t_base + 2)
                        for dc in range(8):
                            sl = slice(dc * T + tc * W, dc * T + tc * W + W)
                            v.tensor_sub(p.vtmp[:, 0:W], p.r[:, sl], p.psA[0][:, 0:W])
                            v.tensor_mul(p.vtmp[:, 0:W], p.vtmp[:, 0:W], p.psA[1][:, 0:W])
                            v.tensor_scalar(xn[:, sl], p.vtmp[:, 0:W],
                                            gcol[:, dc:dc + 1], bcol[:, dc:dc + 1],
                                            ALU.mult, ALU.add).then_inc(p.vS.h, 1)
                            p.vS.n += 1

            for d in "fb":
                stage_a_mamba(p, d, xn, scr_xc[d], scr_sz[d], dbltmp[d])
                evac_dbl(p, d, dbltmp[d], dtvec[d])

        for d in "fb":
            with contextlib.ExitStack() as esd:
                dtwT = esd.enter_context(
                    nc.sbuf_tensor("dtwT_" + d, [64, 16 * 128], BF16))
                prep_dt_tiles(p, d, dtwT)
                stage_c_mamba(p, d, dbltmp[d], dtvec[d], dtwT, state,
                              scr_xc[d], scr_sz[d], scr_y2[d])
            out_proj_mamba(p, d, scr_y2[d], scr_yf)


def xn_mv(p, xn, kc, j0, w, rev):
    if not rev:
        return xn[:, kc * T + j0: kc * T + j0 + w]
    hi = kc * T + (T - 1 - j0)
    lo = kc * T + (T - j0 - w)
    return xn[:, hi: lo - 1 if lo > 0 else None: -1] if lo > 0 else xn[:, hi::-1]


def stage_a_mamba(p, d, xn, sxc, ssz, dbl):
    nc = p.nc
    rev = (d == "b")
    inw = p.wscr["L0_attn_" + d + "_in_w"]
    xpw = p.wscr["L0_attn_" + d + "_xproj_w"]
    cvw = p.convw[d]
    cvb = p.cols["L0_attn_" + d + "_conv_b"]
    W = 512
    with contextlib.ExitStack() as es:
        slab = es.enter_context(nc.sbuf_tensor("ma_slab_" + d, [128, 8 * 1024], BF16))
        xpt = es.enter_context(nc.sbuf_tensor("ma_xpt_" + d, [128, 2 * 96], BF16))
        xip = es.enter_context(nc.sbuf_tensor("ma_xip_" + d, [128, 2 * (3 + T)], F32))
        cvt = es.enter_context(nc.sbuf_tensor("ma_cvt_" + d, [128, 2 * T], F32))
        xcb = es.enter_context(nc.sbuf_tensor("ma_xcb_" + d, [128, T], BF16))

        for q in range(4):
            is_xi = q < 2
            with nc.Block() as blk:
                qw0 = [p.qW[i].n for i in range(4)]
                qy0 = [p.qY[i].n for i in range(2)]
                qx0 = [p.qX[i].n for i in range(2)]
                t_base = p.tS.n
                s_base = p.sS.n
                v_base = p.vS.n
                TPS = 36 if is_xi else 32
                SPS = 6 if is_xi else 4

                @blk.sync
                def _(sync):
                    for kc in range(8):
                        sync.dma_start(slab[:, kc * 1024:(kc + 1) * 1024],
                                       inw[q * 1024:(q + 1) * 1024,
                                           kc * 128:(kc + 1) * 128],
                                       transpose=True).then_inc(p.qW[kc % 4].h, 16)
                        p.qW[kc % 4].n += 16
                    for s in range(8):
                        gs = (q % 2) * 8 + s
                        if is_xi:
                            if s >= 2:
                                sync.wait_ge(p.tS.h, t_base + TPS * (s - 2) + TPS)
                            sync.dma_start(xpt[:, (s % 2) * 96:(s % 2) * 96 + 96],
                                           xpw[0:96, (q * 8 + s) * 128:
                                               (q * 8 + s + 1) * 128],
                                           transpose=True).then_inc(p.qX[s % 2].h, 16)
                            p.qX[s % 2].n += 16
                        sync.wait_ge(p.sS.h, s_base + SPS * s + (5 if is_xi else 4))
                        dst = sxc if is_xi else ssz
                        sync.dma_start(dst[gs * 128:(gs + 1) * 128, :],
                                       cvt[:, (0 if is_xi else (s % 2)) * T:
                                           (0 if is_xi else (s % 2)) * T + T])\
                            .then_inc(p.qY[0].h, 16)
                        p.qY[0].n += 16
                    sync.wait_ge(p.qY[0].h, p.qY[0].n)

                @blk.tensor
                def _(t):
                    for kc in range(8):
                        t.wait_ge(p.qW[kc % 4].h, qw0[kc % 4] + 16 * (kc // 4 + 1))
                    for s in range(8):
                        gs = q * 8 + s
                        for tcc in range(4):
                            u = s * 4 + tcc
                            if u >= 2:
                                u2 = u - 2
                                t.wait_ge(p.sS.h, s_base + (u2 // 4) * SPS + (u2 % 4) + 1)
                            for kc in range(8):
                                t.matmul(p.psA[u % 2][:, 0:W],
                                         slab[:, kc * 1024 + s * 128: kc * 1024 + s * 128 + 128],
                                         xn_mv(p, xn, kc, tcc * W, W, rev),
                                         start=(kc == 0), stop=(kc == 7)).then_inc(p.tS.h, 1)
                                p.tS.n += 1
                        if is_xi:
                            t.wait_ge(p.qX[s % 2].h, qx0[s % 2] + 16 * (s // 2 + 1))
                            t.wait_ge(p.sS.h, s_base + SPS * s + 6)
                            for tcc in range(4):
                                t.matmul(p.psB[0:96, tcc * W:(tcc + 1) * W],
                                         xpt[:, (s % 2) * 96:(s % 2) * 96 + 96],
                                         xcb[:, tcc * W:(tcc + 1) * W],
                                         start=(gs == 0), stop=(gs == 15))\
                                    .then_inc(p.tS.h, 1)
                                p.tS.n += 1

                @blk.scalar
                def _(s_):
                    for s in range(8):
                        if is_xi:
                            if s >= 2:
                                s_.wait_ge(p.vS.h, v_base + (s - 2) + 1)
                            for tcc in range(4):
                                u = s * 4 + tcc
                                s_.wait_ge(p.tS.h, t_base + TPS * s + 8 * (tcc + 1))
                                s_.copy(xip[:, (s % 2) * (3 + T) + 3 + tcc * W:
                                            (s % 2) * (3 + T) + 3 + tcc * W + W],
                                        p.psA[u % 2][:, 0:W]).then_inc(p.sS.h, 1)
                                p.sS.n += 1
                            s_.wait_ge(p.vS.h, v_base + s + 1)
                            if s >= 1:
                                s_.wait_ge(p.tS.h, t_base + TPS * (s - 1) + TPS)
                            s_.activation(cvt[:, 0:T], cvt[:, 0:T], AF.Silu)\
                                .then_inc(p.sS.h, 1)
                            s_.activation(xcb[:, 0:T], cvt[:, 0:T], AF.Copy)\
                                .then_inc(p.sS.h, 1)
                            p.sS.n += 2
                        else:
                            for tcc in range(4):
                                u = s * 4 + tcc
                                s_.wait_ge(p.tS.h, t_base + TPS * s + 8 * (tcc + 1))
                                if tcc == 0 and s >= 2:
                                    s_.wait_ge(p.qY[0].h, qy0[0] + 16 * (s - 1))
                                s_.activation(cvt[:, (s % 2) * T + tcc * W:
                                                  (s % 2) * T + tcc * W + W],
                                              p.psA[u % 2][:, 0:W], AF.Silu)\
                                    .then_inc(p.sS.h, 1)
                                p.sS.n += 1

                if is_xi:
                    @blk.vector
                    def _(v):
                        for s in range(8):
                            gs = q * 8 + s
                            base = (s % 2) * (3 + T)
                            v.memset(xip[:, base:base + 3], 0.0)
                            v.wait_ge(p.sS.h, s_base + SPS * s + 4)
                            if s >= 1:
                                v.wait_ge(p.qY[0].h, qy0[0] + 16 * s)
                                v.wait_ge(p.sS.h, s_base + SPS * (s - 1) + 6)
                            v.tensor_scalar(cvt[:, 0:T], xip[:, base:base + T],
                                            cvw[:, gs * 4:gs * 4 + 1],
                                            cvb[:, gs:gs + 1], ALU.mult, ALU.add)
                            for k in range(1, 4):
                                v.scalar_tensor_tensor(cvt[:, 0:T],
                                                       xip[:, base + k:base + k + T],
                                                       cvw[:, gs * 4 + k:gs * 4 + k + 1],
                                                       cvt[:, 0:T], ALU.mult, ALU.add)
                            v.engine_nop().then_inc(p.vS.h, 1)
                            p.vS.n += 1


def evac_dbl(p, d, dbl, dtv):
    nc = p.nc
    with nc.Block() as blk:
        @blk.scalar
        def _(s):
            for tcc in range(4):
                s.copy(dbl[0:96, tcc * 512:(tcc + 1) * 512],
                       p.psB[0:96, tcc * 512:(tcc + 1) * 512]).then_inc(p.sS.h, 1)
                p.sS.n += 1

        @blk.vector
        def _(v):
            v.wait_ge(p.sS.h, p.sS.n)
            for tcc in range(4):
                v.tensor_copy(dtv[0:64, tcc * 512:(tcc + 1) * 512],
                              dbl[0:64, tcc * 512:(tcc + 1) * 512]).then_inc(p.vS.h, 1)
                p.vS.n += 1


def prep_dt_tiles(p, d, dtw):
    nc = p.nc
    dts = p.wscr["L0_attn_" + d + "_dt_w"]
    with nc.sbuf_tensor("dtstage_" + d, [128, 64], BF16) as dstage:
        with nc.Block() as blk:
            q_base = p.qS.n
            t_base = p.tS.n
            s_base = p.sS.n

            @blk.sync
            def _(sync):
                for s in range(16):
                    if s >= 1:
                        sync.wait_ge(p.tS.h, t_base + s)
                    sync.dma_start(dstage[:, 0:64],
                                   dts[s * 128:(s + 1) * 128, 0:64]).then_inc(p.qS.h, 16)
                    p.qS.n += 16

            @blk.tensor
            def _(t):
                for s in range(16):
                    t.wait_ge(p.qS.h, q_base + 16 * (s + 1))
                    if s >= 2:
                        t.wait_ge(p.sS.h, s_base + s - 1)
                    dstp = (p.psC if s % 2 == 0 else p.psD)
                    t.transpose(dstp[0:64, 0:64].bitcast(BF16), dstage[:, 0:64],
                                p.identb_t[:]).then_inc(p.tS.h, 1)
                    p.tS.n += 1

            @blk.scalar
            def _(s_):
                for s in range(16):
                    s_.wait_ge(p.tS.h, t_base + s + 1)
                    srcp = (p.psC if s % 2 == 0 else p.psD)
                    s_.copy(dtw[0:64, s * 128:(s + 1) * 128],
                            srcp[0:64, 0:64].bitcast(BF16)).then_inc(p.sS.h, 1)
                    p.sS.n += 1


def stage_c_mamba(p, d, dbl, dtv, dtw, state, sxc, ssz, sy2):
    nc = p.nc
    W = 512
    dtb = p.cols["L0_attn_" + d + "_dt_b"]
    Dsk = p.cols["L0_attn_" + d + "_D_skip"]
    Acl = p.Acol[d]
    SPS = 20
    VPS = 20
    TPS = 17
    GPS = 16
    with contextlib.ExitStack() as es:
        bb = es.enter_context(nc.sbuf_tensor("mc_bb_" + d, [128, 32 * W], F32))
        xcs = es.enter_context(nc.sbuf_tensor("mc_xcs_" + d, [128, 2 * W], F32))
        szs = es.enter_context(nc.sbuf_tensor("mc_szs_" + d, [128, 2 * W], F32))
        dlt = es.enter_context(nc.sbuf_tensor("mc_dlt_" + d, [128, W], F32))
        ut = es.enter_context(nc.sbuf_tensor("mc_ut_" + d, [128, W], F32))
        sp1 = es.enter_context(nc.sbuf_tensor("mc_sp1_" + d, [128, W], F32))
        dAt = es.enter_context(nc.sbuf_tensor("mc_dAt_" + d, [128, W], F32))
        dBt = es.enter_context(nc.sbuf_tensor("mc_dBt_" + d, [128, W], F32))
        ht = es.enter_context(nc.sbuf_tensor("mc_ht_" + d, [128, 2 * W], F32))
        pbf = es.enter_context(nc.sbuf_tensor("mc_pbf_" + d, [128, 2 * W], BF16))
        y2t = es.enter_context(nc.sbuf_tensor("mc_y2t_" + d, [128, 2 * W], BF16))

        with nc.Block() as blk:
            @blk.vector
            def _(v):
                v.memset(state[:], 0.0)

        for tcc in range(4):
            with nc.Block() as blk:
                t_base = p.tS.n
                s_base = p.sS.n

                @blk.tensor
                def _(t):
                    for n in range(32):
                        if n >= 2:
                            t.wait_ge(p.sS.h, s_base + n - 1)
                        sel = p.identf_t[0:96, 64 + n:65 + n].broadcast_to([96, 128])
                        t.matmul(p.psA[n % 2][:, 0:W], sel,
                                 dbl[0:96, tcc * W:(tcc + 1) * W],
                                 start=True, stop=True).then_inc(p.tS.h, 1)
                        p.tS.n += 1

                @blk.scalar
                def _(s_):
                    for n in range(32):
                        s_.wait_ge(p.tS.h, t_base + n + 1)
                        s_.copy(bb[:, n * W:(n + 1) * W], p.psA[n % 2][:, 0:W])\
                            .then_inc(p.sS.h, 1)
                        p.sS.n += 1

            with nc.Block() as blk:
                qx_base = [p.qX[i].n for i in range(2)]
                qz_base = [p.qZ[i].n for i in range(2)]
                qy_base = [p.qY[i].n for i in range(2)]
                t_base = p.tS.n
                s_base = p.sS.n
                v_base = p.vS.n
                g_base = p.gS.n

                @blk.sync
                def _(sync):
                    for s in range(16):
                        sl = s % 2
                        if s >= 2:
                            sync.wait_ge(p.vS.h, v_base + VPS * (s - 2) + VPS)
                        sync.dma_start(xcs[:, sl * W:(sl + 1) * W],
                                       sxc[s * 128:(s + 1) * 128, tcc * W:(tcc + 1) * W])\
                            .then_inc(p.qX[sl].h, 16)
                        p.qX[sl].n += 16
                        sync.dma_start(szs[:, sl * W:(sl + 1) * W],
                                       ssz[s * 128:(s + 1) * 128, tcc * W:(tcc + 1) * W])\
                            .then_inc(p.qZ[sl].h, 16)
                        p.qZ[sl].n += 16
                        sync.wait_ge(p.vS.h, v_base + VPS * s + VPS)
                        sync.dma_start(sy2[s * 128:(s + 1) * 128, tcc * W:(tcc + 1) * W],
                                       y2t[:, sl * W:(sl + 1) * W]).then_inc(p.qY[sl].h, 16)
                        p.qY[sl].n += 16
                    sync.wait_ge(p.qY[0].h, p.qY[0].n)
                    sync.wait_ge(p.qY[1].h, p.qY[1].n)

                @blk.tensor
                def _(t):
                    for s in range(16):
                        if s >= 2:
                            t.wait_ge(p.sS.h, s_base + SPS * (s - 2) + 4)
                        t.matmul(p.psA[s % 2][:, 0:W], dtw[0:64, s * 128:(s + 1) * 128],
                                 dtv[0:64, tcc * W:(tcc + 1) * W],
                                 start=True, stop=True).then_inc(p.tS.h, 1)
                        p.tS.n += 1
                        for n in range(16):
                            t.wait_ge(p.gS.h, g_base + GPS * s + n + 1)
                            if n == 0 and s >= 1:
                                t.wait_ge(p.vS.h, v_base + VPS * (s - 1) + 19)
                            t.matmul(p.psC[:, 0:W], p.identb_t[:],
                                     pbf[:, (n % 2) * W:(n % 2) * W + W],
                                     start=(n == 0), stop=(n == 15)).then_inc(p.tS.h, 1)
                            p.tS.n += 1

                @blk.scalar
                def _(s_):
                    for s in range(16):
                        sb = s_base + SPS * s
                        if s >= 1:
                            s_.wait_ge(p.vS.h, v_base + VPS * (s - 1) + 2)
                        s_.wait_ge(p.tS.h, t_base + TPS * s + 1)
                        s_.activation(sp1[:, 0:W], p.psA[s % 2][:, 0:W], AF.Abs,
                                      bias=dtb[:, s:s + 1]).then_inc(p.sS.h, 1)
                        s_.activation(sp1[:, 0:W], sp1[:, 0:W], AF.Exp, scale=-1.0)\
                            .then_inc(p.sS.h, 1)
                        s_.activation(sp1[:, 0:W], sp1[:, 0:W], AF.Ln, bias=1.0)\
                            .then_inc(p.sS.h, 1)
                        s_.activation(dlt[:, 0:W], p.psA[s % 2][:, 0:W], AF.Relu,
                                      bias=dtb[:, s:s + 1]).then_inc(p.sS.h, 1)
                        p.sS.n += 4
                        s_.wait_ge(p.vS.h, v_base + VPS * s + 1)
                        for n in range(16):
                            if n >= 1:
                                s_.wait_ge(p.vS.h, v_base + VPS * s + 2 + n)
                            s_.activation(dAt[:, 0:W], dlt[:, 0:W],
                                          AF.Exp, scale=Acl[:, s * 16 + n:s * 16 + n + 1])\
                                .then_inc(p.sS.h, 1)
                            p.sS.n += 1

                @blk.vector
                def _(v):
                    for s in range(16):
                        sl = s % 2
                        vb = v_base + VPS * s
                        sb = s_base + SPS * s
                        v.wait_ge(p.sS.h, sb + 4)
                        v.tensor_add(dlt[:, 0:W], dlt[:, 0:W], sp1[:, 0:W])\
                            .then_inc(p.vS.h, 1)
                        v.wait_ge(p.qX[sl].h, qx_base[sl] + 16 * (s // 2 + 1))
                        v.tensor_mul(ut[:, 0:W], dlt[:, 0:W], xcs[:, sl * W:(sl + 1) * W])\
                            .then_inc(p.vS.h, 1)
                        p.vS.n += 2
                        for n in range(16):
                            v.tensor_mul(dBt[:, 0:W], ut[:, 0:W], bb[:, n * W:(n + 1) * W])
                            v.wait_ge(p.sS.h, sb + 4 + n + 1)
                            idx = s * 16 + n
                            v.scalar_tensor_tensor(dBt[:, 0:1], dAt[:, 0:1],
                                                   state[:, idx:idx + 1], dBt[:, 0:1],
                                                   ALU.mult, ALU.add)
                            k = GPS * s + n
                            if k >= 2:
                                v.wait_ge(p.gS.h, g_base + k - 1)
                            v.tensor_tensor_scan(ht[:, (n % 2) * W:(n % 2) * W + W],
                                                 dAt[:, 0:W],
                                                 dBt[:, 0:W], 0.0,
                                                 ALU.mult, ALU.add).then_inc(p.vS.h, 1)
                            v.tensor_copy(state[:, idx:idx + 1],
                                          ht[:, (n % 2) * W + W - 1:(n % 2) * W + W])
                            p.vS.n += 1
                        v.wait_ge(p.tS.h, t_base + TPS * s + TPS)
                        v.scalar_tensor_tensor(ut[:, 0:W], xcs[:, sl * W:(sl + 1) * W],
                                               Dsk[:, s:s + 1], p.psC[:, 0:W],
                                               ALU.mult, ALU.add).then_inc(p.vS.h, 1)
                        v.wait_ge(p.qZ[sl].h, qz_base[sl] + 16 * (s // 2 + 1))
                        if s >= 2:
                            v.wait_ge(p.qY[sl].h, qy_base[sl] + 16 * (s // 2))
                        v.tensor_mul(y2t[:, sl * W:(sl + 1) * W], ut[:, 0:W],
                                     szs[:, sl * W:(sl + 1) * W]).then_inc(p.vS.h, 1)
                        p.vS.n += 2

                @blk.gpsimd
                def _(g):
                    for s in range(16):
                        for n in range(16):
                            k = GPS * s + n
                            g.wait_ge(p.vS.h, v_base + VPS * s + 2 + n + 1)
                            if k >= 2:
                                kk = k - 2
                                g.wait_ge(p.tS.h, t_base + TPS * (kk // 16)
                                          + 1 + (kk % 16) + 1)
                            g.tensor_mul(pbf[:, (n % 2) * W:(n % 2) * W + W],
                                         ht[:, (n % 2) * W:(n % 2) * W + W],
                                         bb[:, (16 + n) * W:(17 + n) * W])\
                                .then_inc(p.gS.h, 1)
                            p.gS.n += 1


def out_proj_mamba(p, d, sy2, syf):
    nc = p.nc
    W = 512
    ows = p.wscr["L0_attn_" + d + "_out_w"]
    rev = (d == "b")
    with contextlib.ExitStack() as es:
        slabs = es.enter_context(nc.sbuf_tensor("op_slabs_" + d, [128, 16 * 1024], BF16))
        y2b = es.enter_context(nc.sbuf_tensor("op_y2b_" + d, [128, 16 * W], BF16))
        opl = es.enter_context(nc.sbuf_tensor("op_opl_" + d, [128, 8 * W], F32))
        yfb = es.enter_context(nc.sbuf_tensor("op_yfb_" + d, [128, 8 * W], F32))

        for tcc in range(4):
            ntc = 3 - tcc if rev else tcc
            with nc.Block() as blk:
                @blk.sync
                def _(sync):
                    for kc in range(16):
                        sync.dma_start(slabs[:, kc * 1024:(kc + 1) * 1024],
                                       ows[0:1024, kc * 128:(kc + 1) * 128],
                                       transpose=True).then_inc(p.qS.h, 16)
                        p.qS.n += 16
                        sync.dma_start(y2b[:, kc * W:(kc + 1) * W],
                                       sy2[kc * 128:(kc + 1) * 128, tcc * W:(tcc + 1) * W])\
                            .then_inc(p.qS.h, 16)
                        p.qS.n += 16
                    if rev:
                        for m in range(8):
                            sync.dma_start(yfb[:, m * W:(m + 1) * W],
                                           syf[m * 128:(m + 1) * 128,
                                               ntc * W:(ntc + 1) * W]).then_inc(p.qS.h, 16)
                            p.qS.n += 16
                    sync.wait_ge(p.qS.h, p.qS.n)

            with nc.Block() as blk:
                t_base = p.tS.n
                s_base = p.sS.n
                v_base = p.vS.n
                q_base = p.qS.n

                @blk.tensor
                def _(t):
                    for m in range(8):
                        if m >= 2:
                            if rev:
                                t.wait_ge(p.vS.h, v_base + m - 1)
                            else:
                                t.wait_ge(p.sS.h, s_base + m - 1)
                        for kc in range(16):
                            t.matmul(p.psA[m % 2][:, 0:W],
                                     slabs[:, kc * 1024 + m * 128: kc * 1024 + m * 128 + 128],
                                     y2b[:, kc * W:(kc + 1) * W],
                                     start=(kc == 0), stop=(kc == 15)).then_inc(p.tS.h, 1)
                            p.tS.n += 1

                if not rev:
                    @blk.scalar
                    def _(s_):
                        for m in range(8):
                            s_.wait_ge(p.tS.h, t_base + 16 * (m + 1))
                            s_.activation(opl[:, m * W:(m + 1) * W], p.psA[m % 2][:, 0:W],
                                          AF.Copy, scale=0.5).then_inc(p.sS.h, 1)
                            p.sS.n += 1

                    @blk.sync
                    def _(sync):
                        for m in range(8):
                            sync.wait_ge(p.sS.h, s_base + m + 1)
                            sync.dma_start(syf[m * 128:(m + 1) * 128, tcc * W:(tcc + 1) * W],
                                           opl[:, m * W:(m + 1) * W]).then_inc(p.qS.h, 16)
                            p.qS.n += 16
                        sync.wait_ge(p.qS.h, p.qS.n)
                else:
                    @blk.vector
                    def _(v):
                        for m in range(8):
                            v.wait_ge(p.tS.h, t_base + 16 * (m + 1))
                            v.scalar_tensor_tensor(opl[:, m * W:(m + 1) * W],
                                                   p.psA[m % 2][:, W - 1::-1], 0.5,
                                                   yfb[:, m * W:(m + 1) * W],
                                                   ALU.mult, ALU.add).then_inc(p.vS.h, 1)
                            p.vS.n += 1

            if rev:
                post_ln_residual_tc(p, ntc, lambda dc: opl[:, dc * W:(dc + 1) * W])


def build_l1_attn(p):
    nc = p.nc
    TC = 256
    LNK = FC(math.log(1.0 / 16.0))
    with contextlib.ExitStack() as es:
        xn3 = es.enter_context(nc.sbuf_tensor("ga_xn", [128, 8 * T], BF16))
        qpl = es.enter_context(nc.sbuf_tensor("ga_q", [128, 8 * TC], BF16))
        kpl = es.enter_context(nc.sbuf_tensor("ga_k", [128, 8 * TC], BF16))
        vpl = es.enter_context(nc.sbuf_tensor("ga_v", [128, 2 * 2048], BF16))
        gsc = es.enter_context(nc.sbuf_tensor("ga_gsc", [128, 16 * TC], F32))
        opl = es.enter_context(nc.sbuf_tensor("ga_o", [128, 16 * TC], F32))
        Spl = es.enter_context(nc.sbuf_tensor("ga_S", [128, 4096], F32))
        Sb = es.enter_context(nc.sbuf_tensor("ga_Sb", [128, 4096], BF16))
        wsl = es.enter_context(nc.sbuf_tensor("ga_wsl", [128, 4096], BF16))
        aol = gsc
        gk1 = es.enter_context(nc.sbuf_tensor("ga_gk1", [16, TC], BF16))
        e1 = es.enter_context(nc.sbuf_tensor("ga_e1", [128, 128], F32))
        e2 = es.enter_context(nc.sbuf_tensor("ga_e2", [128, 128], F32))
        kdf = es.enter_context(nc.sbuf_tensor("ga_kdf", [128, 128], F32))
        qgb = es.enter_context(nc.sbuf_tensor("ga_qgb", [128, 128], BF16))
        kdb = es.enter_context(nc.sbuf_tensor("ga_kdb", [128, 128], BF16))
        kgb = es.enter_context(nc.sbuf_tensor("ga_kgb", [128, 128], BF16))
        atb = es.enter_context(nc.sbuf_tensor("ga_atb", [128, 64], BF16))
        kgtb = es.enter_context(nc.sbuf_tensor("ga_kgtb", [128, 256], BF16))
        e4c = es.enter_context(nc.sbuf_tensor("ga_e4c", [128, 8], F32))
        onest = es.enter_context(nc.sbuf_tensor("ga_ones", [128, 64], F32))
        bgn = es.enter_context(nc.sbuf_tensor("ga_bgn", [128, 8], F32))

        plain_preln_plane(p, xn3)

        with nc.Block() as blk:
            @blk.vector
            def _(v):
                v.memset(Spl[:], 0.0)
                v.memset(Sb[:], 0.0)
                v.memset(onest[:], 1.0)
                v.tensor_scalar(bgn[:], p.cols["L1_attn_bgk2"][:], -1.0, None, ALU.mult)

        if DEBUG_GLA:
            dbg = {}
            for nm, shp, dt in [("g_q", [128, 8 * TC], BF16), ("g_k", [128, 8 * TC], BF16),
                                ("g_v", [128, 2 * 2048], BF16), ("g_gsc", [128, 16 * TC], F32),
                                ("g_opl", [128, 16 * TC], F32), ("g_aol", [128, 8 * TC], F32)]:
                dbg[nm] = nc.dram_tensor(nm, shp, dt, kind="ExternalOutput").ap()
        for tc in range(T // TC):
            gla_proj(p, tc, xn3, qpl, kpl, vpl, gsc, gk1, wsl, bgn, onest)
            if DEBUG_GLA and tc == 0:
                with nc.Block() as blk:
                    @blk.sync
                    def _(sync):
                        for nm, src_t in [("g_q", qpl), ("g_k", kpl), ("g_v", vpl),
                                          ("g_gsc", gsc)]:
                            sync.dma_start(dbg[nm], src_t[:]).then_inc(p.qS.h, 16)
                            p.qS.n += 16
                        sync.wait_ge(p.qS.h, p.qS.n)
            gla_inner(p, tc, qpl, kpl, vpl, gsc, opl, Spl, Sb,
                      e1, e2, kdf, qgb, kdb, kgb, atb, kgtb, e4c, LNK)
            if DEBUG_GLA and tc == 0:
                with nc.Block() as blk:
                    @blk.sync
                    def _(sync):
                        sync.dma_start(dbg["g_opl"], opl[:]).then_inc(p.qS.h, 16)
                        p.qS.n += 16
                        sync.wait_ge(p.qS.h, p.qS.n)
            gla_final(p, tc, xn3, opl, gsc, aol, wsl, qpl, kpl)
            if DEBUG_GLA and tc == 0:
                with nc.Block() as blk:
                    @blk.sync
                    def _(sync):
                        sync.dma_start(dbg["g_aol"], aol[:, 0:8 * TC]).then_inc(p.qS.h, 16)
                        p.qS.n += 16
                        sync.wait_ge(p.qS.h, p.qS.n)
            post_ln_residual_tc(p, tc, lambda dc: aol[:, dc * TC:(dc + 1) * TC],
                                tcw=TC)


def gla_proj(p, tc, xn3, qpl, kpl, vpl, gsc, gk1, wsl, bgn, onest):
    nc = p.nc
    TC, CH = 256, 64

    def xmov(kc, w=TC, off=0):
        return xn3[:, kc * T + tc * TC + off: kc * T + tc * TC + off + w]

    for wname, outpl in [("L1_attn_Wq", qpl), ("L1_attn_Wk", kpl)]:
        wscr = p.wscr[wname]
        with nc.Block() as blk:
            t_base = p.tS.n
            s_base = p.sS.n
            qw_base = [p.qW[i].n for i in range(4)]

            @blk.sync
            def _(sync):
                for m in range(8):
                    for kc in range(8):
                        j = m * 8 + kc
                        slot = j % 4
                        if j >= 4:
                            sync.wait_ge(p.tS.h, t_base + (j - 3))
                        sync.dma_start(wsl[:, slot * 128:(slot + 1) * 128],
                                       wscr[kc * 128:(kc + 1) * 128,
                                            m * 128:(m + 1) * 128])\
                            .then_inc(p.qW[slot].h, 16)
                        p.qW[slot].n += 16

            @blk.tensor
            def _(t):
                for m in range(8):
                    if m >= 2:
                        t.wait_ge(p.sS.h, s_base + (m - 1))
                    for kc in range(8):
                        j = m * 8 + kc
                        slot = j % 4
                        t.wait_ge(p.qW[slot].h, qw_base[slot] + 16 * (j // 4 + 1))
                        t.matmul(p.psA[m % 2][:, 0:TC],
                                 wsl[:, slot * 128:(slot + 1) * 128], xmov(kc),
                                 start=(kc == 0), stop=(kc == 7)).then_inc(p.tS.h, 1)
                        p.tS.n += 1

            @blk.scalar
            def _(s_):
                for m in range(8):
                    s_.wait_ge(p.tS.h, t_base + 8 * (m + 1))
                    s_.copy(outpl[:, m * TC:(m + 1) * TC],
                            p.psA[m % 2][:, 0:TC]).then_inc(p.sS.h, 1)
                    p.sS.n += 1

    with nc.Block() as blk:
        q_base = p.qS.n
        t_base = p.tS.n

        @blk.sync
        def _(sync):
            for kc in range(8):
                sync.dma_start(wsl[:, kc * 16:(kc + 1) * 16],
                               p.wscr["L1_attn_Wgk1"][kc * 128:(kc + 1) * 128, 0:16])\
                    .then_inc(p.qS.h, 16)
                p.qS.n += 16
            sync.wait_ge(p.qS.h, p.qS.n)

        @blk.tensor
        def _(t):
            t.wait_ge(p.qS.h, q_base + 16 * 8)
            for kc in range(8):
                t.matmul(p.psD[0:16, 0:TC], wsl[:, kc * 16:(kc + 1) * 16], xmov(kc),
                         start=(kc == 0), stop=(kc == 7)).then_inc(p.tS.h, 1)
                p.tS.n += 1

        @blk.scalar
        def _(s_):
            s_.wait_ge(p.tS.h, t_base + 8)
            s_.copy(gk1[0:16, 0:TC], p.psD[0:16, 0:TC]).then_inc(p.sS.h, 1)
            p.sS.n += 1

    with nc.Block() as blk:
        q_base = p.qS.n
        t_base = p.tS.n
        s_base = p.sS.n
        v_base = p.vS.n

        @blk.sync
        def _(sync):
            sync.dma_start(wsl[0:16, 0:1024], p.wscr["L1_attn_Wgk2"][0:16, 0:1024])\
                .then_inc(p.qS.h, 16)
            p.qS.n += 16
            sync.wait_ge(p.qS.h, p.qS.n)

        @blk.tensor
        def _(t):
            t.wait_ge(p.qS.h, q_base + 16)
            for m in range(8):
                if m >= 2:
                    t.wait_ge(p.sS.h, s_base + 4 * (m - 1))
                t.matmul(p.psA[m % 2][:, 0:TC], wsl[0:16, m * 128:(m + 1) * 128],
                         gk1[0:16, 0:TC], start=True, stop=True).then_inc(p.tS.h, 1)
                p.tS.n += 1

        @blk.scalar
        def _(s_):
            for m in range(8):
                s_.wait_ge(p.tS.h, t_base + m + 1)
                if m >= 1:
                    s_.wait_ge(p.vS.h, v_base + m)
                s_.activation(p.sqtile[:, 0:TC], p.psA[m % 2][:, 0:TC], AF.Abs,
                              bias=p.cols["L1_attn_bgk2"][:, m:m + 1]).then_inc(p.sS.h, 1)
                s_.activation(p.sqtile[:, 0:TC], p.sqtile[:, 0:TC], AF.Exp, scale=-1.0)\
                    .then_inc(p.sS.h, 1)
                s_.activation(p.sqtile[:, 0:TC], p.sqtile[:, 0:TC], AF.Ln, bias=1.0)\
                    .then_inc(p.sS.h, 1)
                s_.activation(gsc[:, (8 + m) * TC:(9 + m) * TC],
                              p.psA[m % 2][:, 0:TC], AF.Relu,
                              scale=-1.0, bias=bgn[:, m:m + 1]).then_inc(p.sS.h, 1)
                p.sS.n += 4

        @blk.vector
        def _(v):
            for m in range(8):
                v.wait_ge(p.sS.h, s_base + 4 * (m + 1))
                v.tensor_add(gsc[:, (8 + m) * TC:(9 + m) * TC],
                             gsc[:, (8 + m) * TC:(9 + m) * TC],
                             p.sqtile[:, 0:TC]).then_inc(p.vS.h, 1)
                p.vS.n += 1
            for m in range(8):
                for c in range(4):
                    sl = slice((8 + m) * TC + c * CH, (8 + m) * TC + (c + 1) * CH)
                    v.tensor_tensor_scan(gsc[:, sl], onest[:, 0:CH], gsc[:, sl],
                                         0.0, ALU.mult, ALU.add).then_inc(p.vS.h, 1)
                    p.vS.n += 1

    with nc.Block() as blk:
        t_base = p.tS.n
        qv_base = [p.qW[i].n for i in range(2)]

        def psv(tt, dvs):
            i = tt * 4 + dvs
            if i < 2:
                return p.psA[i][:, 0:512]
            if i < 6:
                return p.psB[:, (i - 2) * 512:(i - 1) * 512]
            return (p.psC if i == 6 else p.psD)[:, 0:512]

        @blk.sync
        def _(sync):
            for kc in range(8):
                slot = kc % 2
                if kc >= 2:
                    sync.wait_ge(p.tS.h, t_base + 8 * (kc - 1))
                sync.dma_start(wsl[:, slot * 2048:(slot + 1) * 2048],
                               p.wscr["L1_attn_Wv"][kc * 128:(kc + 1) * 128, 0:2048])\
                    .then_inc(p.qW[slot].h, 16)
                p.qW[slot].n += 16

        @blk.tensor
        def _(t):
            for kc in range(8):
                slot = kc % 2
                t.wait_ge(p.qW[slot].h, qv_base[slot] + 16 * (kc // 2 + 1))
                for tt in range(2):
                    for dvs in range(4):
                        t.matmul(psv(tt, dvs), xmov(kc, 128, tt * 128),
                                 wsl[:, slot * 2048 + dvs * 512:
                                     slot * 2048 + (dvs + 1) * 512],
                                 start=(kc == 0), stop=(kc == 7)).then_inc(p.tS.h, 1)
                        p.tS.n += 1

        @blk.scalar
        def _(s_):
            s_.wait_ge(p.tS.h, t_base + 64)
            for tt in range(2):
                for dvs in range(4):
                    s_.copy(vpl[:, tt * 2048 + dvs * 512: tt * 2048 + (dvs + 1) * 512],
                            psv(tt, dvs)).then_inc(p.sS.h, 1)
                    p.sS.n += 1


def gla_inner(p, tc, qpl, kpl, vpl, gsc, opl, Spl, Sb,
              e1, e2, kdf, qgb, kdb, kgb, atb, kgtb, e4c, LNK):
    nc = p.nc
    TC, CH = 256, 64
    SP = 11
    VP = 8
    TP = 18
    with nc.Block() as blk:
        t_base = p.tS.n
        s_base = p.sS.n
        v_base = p.vS.n

        def sbase(c, h):
            return s_base + (c + 1) + SP * (4 * c + h)

        @blk.scalar
        def _(s_):
            for c in range(4):
                ap = gsc[:, 8 * TC + c * CH + CH - 1:: TC]
                if c >= 1:
                    s_.wait_ge(p.vS.h, v_base + VP * (4 * c))
                s_.activation(e4c[:, 0:8], ap, AF.Exp, scale=FC(-1.0 / 16.0))\
                    .then_inc(p.sS.h, 1)
                p.sS.n += 1
                for h in range(4):
                    khc = 4 * c + h
                    sb = sbase(c, h)
                    if khc >= 1:
                        s_.wait_ge(p.vS.h, v_base + VP * khc)
                    for ph in range(2):
                        sl = slice((8 + 2 * h + ph) * TC + c * CH,
                                   (8 + 2 * h + ph) * TC + (c + 1) * CH)
                        s_.activation(e1[:, ph * CH:(ph + 1) * CH], gsc[:, sl], AF.Exp,
                                      scale=FC(-1.0 / 16.0), bias=LNK).then_inc(p.sS.h, 1)
                        s_.activation(e2[:, ph * CH:(ph + 1) * CH], gsc[:, sl], AF.Exp,
                                      scale=FC(1.0 / 16.0)).then_inc(p.sS.h, 1)
                        p.sS.n += 2
                    s_.wait_ge(p.tS.h, t_base + TP * khc + 14)
                    for dvc in range(4):
                        s_.copy(opl[:, (4 * h + dvc) * TC + c * CH:
                                    (4 * h + dvc) * TC + (c + 1) * CH],
                                p.psA[0][:, dvc * CH:(dvc + 1) * CH]).then_inc(p.sS.h, 1)
                        p.sS.n += 1
                    s_.wait_ge(p.tS.h, t_base + TP * khc + 16)
                    row_s = (c % 2) * 64
                    for i in range(2):
                        s_.copy(kgtb[row_s:row_s + 64, i * 128:(i + 1) * 128],
                                p.psA[1][row_s:row_s + 64,
                                         128 + i * 64:128 + (i + 1) * 64]
                                .bitcast(BF16)).then_inc(p.sS.h, 1)
                        p.sS.n += 1
                    s_.wait_ge(p.vS.h, v_base + VP * khc + VP)
                    s_.copy(Sb[:, h * 1024:(h + 1) * 1024],
                            Spl[:, h * 1024:(h + 1) * 1024]).then_inc(p.sS.h, 1)
                    p.sS.n += 1

        @blk.vector
        def _(v):
            for c in range(4):
                for h in range(4):
                    khc = 4 * c + h
                    vb = v_base + VP * khc
                    sb = sbase(c, h)
                    v.wait_ge(p.sS.h, sb + 4)
                    if khc >= 1:
                        v.wait_ge(p.tS.h, t_base + TP * (khc - 1) + 16)
                    for ph in range(2):
                        qs = qpl[:, (2 * h + ph) * TC + c * CH:
                                 (2 * h + ph) * TC + (c + 1) * CH]
                        ks = kpl[:, (2 * h + ph) * TC + c * CH:
                                 (2 * h + ph) * TC + (c + 1) * CH]
                        v.tensor_mul(qgb[:, ph * CH:(ph + 1) * CH], qs,
                                     e1[:, ph * CH:(ph + 1) * CH])
                        v.tensor_mul(kdf[:, ph * CH:(ph + 1) * CH], ks,
                                     e2[:, ph * CH:(ph + 1) * CH])
                        v.tensor_copy(kdb[:, ph * CH:(ph + 1) * CH],
                                      kdf[:, ph * CH:(ph + 1) * CH])
                        v.tensor_scalar(kgb[:, ph * CH:(ph + 1) * CH],
                                        kdf[:, ph * CH:(ph + 1) * CH],
                                        e4c[:, 2 * h + ph:2 * h + ph + 1], None,
                                        ALU.mult)
                    v.engine_nop().then_inc(p.vS.h, 1)
                    p.vS.n += 1
                    row = (c % 2) * 64
                    v.wait_ge(p.tS.h, t_base + TP * khc + 2)
                    v.tensor_mul(atb[row:row + 64, 0:64],
                                 p.psA[1][row:row + 64, 0:64],
                                 p.maskut_t[row:row + 64, 0:64]).then_inc(p.vS.h, 1)
                    p.vS.n += 1
                    v.wait_ge(p.tS.h, t_base + TP * khc + TP)
                    for ph in range(2):
                        ssl = slice(h * 1024 + ph * 512, h * 1024 + (ph + 1) * 512)
                        v.tensor_scalar(Spl[:, ssl], Spl[:, ssl],
                                        e4c[:, 2 * h + ph:2 * h + ph + 1], None,
                                        ALU.mult).then_inc(p.vS.h, 1)
                        v.tensor_add(Spl[:, ssl], Spl[:, ssl],
                                     p.psB[:, ph * 512:(ph + 1) * 512])\
                            .then_inc(p.vS.h, 2)
                        p.vS.n += 3

        @blk.tensor
        def _(t):
            for c in range(4):
                for h in range(4):
                    khc = 4 * c + h
                    vb = v_base + VP * khc
                    sb = sbase(c, h)
                    tt, row = c // 2, (c % 2) * 64
                    t.wait_ge(p.vS.h, vb + 1)
                    if khc >= 1:
                        t.wait_ge(p.sS.h, sbase(c, h - 1) + 10 if h > 0
                                  else sbase(c - 1, 3) + 10)
                    for ph in range(2):
                        t.matmul(p.psA[1][row:row + 64, 0:64],
                                 kdb[:, ph * CH:(ph + 1) * CH],
                                 qgb[:, ph * CH:(ph + 1) * CH],
                                 start=(ph == 0), stop=(ph == 1),
                                 tile_position=(0, row)).then_inc(p.tS.h, 1)
                        p.tS.n += 1
                    t.wait_ge(p.vS.h, vb + 2)
                    if khc >= 1:
                        t.wait_ge(p.sS.h, (sbase(c, h - 1) if h > 0
                                           else sbase(c - 1, 3)) + 8)
                    for dvc in range(4):
                        t.matmul(p.psA[0][:, dvc * CH:(dvc + 1) * CH],
                                 vpl[row:row + 64,
                                     tt * 2048 + h * 512 + dvc * 128:
                                     tt * 2048 + h * 512 + (dvc + 1) * 128],
                                 atb[row:row + 64, 0:64], start=True, stop=False)\
                            .then_inc(p.tS.h, 1)
                        p.tS.n += 1
                        for ph in range(2):
                            t.matmul(p.psA[0][:, dvc * CH:(dvc + 1) * CH],
                                     Sb[:, h * 1024 + ph * 512 + dvc * 128:
                                        h * 1024 + ph * 512 + (dvc + 1) * 128],
                                     qgb[:, ph * CH:(ph + 1) * CH],
                                     start=False, stop=(ph == 1)).then_inc(p.tS.h, 1)
                            p.tS.n += 1
                    t.wait_ge(p.vS.h, vb + 2)
                    for i in range(2):
                        t.transpose(p.psA[1][row:row + 64,
                                             128 + i * 64:128 + (i + 1) * 64]
                                    .bitcast(BF16),
                                    kgb[:, i * CH:(i + 1) * CH], p.identb_t[:],
                                    tile_position=(0, row)).then_inc(p.tS.h, 1)
                        p.tS.n += 1
                    t.wait_ge(p.sS.h, sb + 10)
                    if khc >= 1:
                        t.wait_ge(p.vS.h, vb - VP + VP)
                    for ph in range(2):
                        t.matmul(p.psB[:, ph * 512:(ph + 1) * 512],
                                 kgtb[row:row + 64, ph * 128:(ph + 1) * 128],
                                 vpl[row:row + 64, tt * 2048 + h * 512:
                                     tt * 2048 + (h + 1) * 512],
                                 start=True, stop=True).then_inc(p.tS.h, 1)
                        p.tS.n += 1


def gla_final(p, tc, xn3, opl, gsc, aol, wsl, qpl, kpl):
    nc = p.nc
    TC = 256

    def xmov(kc):
        return xn3[:, kc * T + tc * TC: kc * T + tc * TC + TC]

    with nc.Block() as blk:
        t_base = p.tS.n
        s_base = p.sS.n
        qw_base = [p.qW[i].n for i in range(4)]

        @blk.sync
        def _(sync):
            for m in range(16):
                for kc in range(8):
                    j = m * 8 + kc
                    slot = j % 4
                    if j >= 4:
                        sync.wait_ge(p.tS.h, t_base + (j - 3))
                    sync.dma_start(wsl[:, slot * 128:(slot + 1) * 128],
                                   p.wscr["L1_attn_Wg"][kc * 128:(kc + 1) * 128,
                                                        m * 128:(m + 1) * 128])\
                        .then_inc(p.qW[slot].h, 16)
                    p.qW[slot].n += 16

        @blk.tensor
        def _(t):
            for m in range(16):
                if m >= 2:
                    t.wait_ge(p.sS.h, s_base + (m - 1))
                for kc in range(8):
                    j = m * 8 + kc
                    slot = j % 4
                    t.wait_ge(p.qW[slot].h, qw_base[slot] + 16 * (j // 4 + 1))
                    t.matmul(p.psA[m % 2][:, 0:TC],
                             wsl[:, slot * 128:(slot + 1) * 128], xmov(kc),
                             start=(kc == 0), stop=(kc == 7)).then_inc(p.tS.h, 1)
                    p.tS.n += 1

        @blk.scalar
        def _(s_):
            for m in range(16):
                s_.wait_ge(p.tS.h, t_base + 8 * (m + 1))
                s_.activation(gsc[:, m * TC:(m + 1) * TC], p.psA[m % 2][:, 0:TC],
                              AF.Silu).then_inc(p.sS.h, 1)
                p.sS.n += 1

    with nc.Block() as blk:
        t_base = p.tS.n
        s_base = p.sS.n
        v_base = p.vS.n
        rwc = p.cols["L1_attn_rms_w"]

        @blk.scalar
        def _(s_):
            for h in range(4):
                for dvc in range(4):
                    i = 4 * h + dvc
                    if i >= 2:
                        s_.wait_ge(p.tS.h, t_base + 5 * ((i - 2) // 4) + (i - 2) % 4 + 1)
                    sq = p.sqtile if dvc % 2 == 0 else p.vtmp
                    s_.activation(sq[:, 0:TC], opl[:, i * TC:(i + 1) * TC],
                                  AF.Square).then_inc(p.sS.h, 1)
                    p.sS.n += 1
                s_.wait_ge(p.vS.h, v_base + 7 * h + 1)
                s_.activation(p.row_b[0:1, 0:TC], p.row_a[0:1, 0:TC], AF.Sqrt)\
                    .then_inc(p.sS.h, 1)
                p.sS.n += 1

        @blk.tensor
        def _(t):
            for h in range(4):
                for dvc in range(4):
                    t.wait_ge(p.sS.h, s_base + 5 * h + dvc + 1)
                    if dvc == 0 and h >= 1:
                        t.wait_ge(p.vS.h, v_base + 7 * (h - 1) + 1)
                    sq = p.sqtile if dvc % 2 == 0 else p.vtmp
                    t.matmul(p.psC[0:1, 0:TC], p.ones_col[:, 0:1], sq[:, 0:TC],
                             start=(dvc == 0), stop=(dvc == 3)).then_inc(p.tS.h, 1)
                    p.tS.n += 1
                t.wait_ge(p.vS.h, v_base + 7 * h + 2)
                t.matmul(p.psD[:, 0:TC], p.ones_row[0:1, :], p.row_rs[0:1, 0:TC],
                         start=True, stop=True).then_inc(p.tS.h, 1)
                p.tS.n += 1

        @blk.vector
        def _(v):
            for h in range(4):
                v.wait_ge(p.tS.h, t_base + 5 * h + 4)
                v.tensor_scalar(p.row_a[0:1, 0:TC], p.psC[0:1, 0:TC],
                                FC(1.0 / 512.0), FC(EPS), ALU.mult, ALU.add)\
                    .then_inc(p.vS.h, 1)
                p.vS.n += 1
                v.wait_ge(p.sS.h, s_base + 5 * (h + 1))
                v.reciprocal(p.row_rs[0:1, 0:TC], p.row_b[0:1, 0:TC])\
                    .then_inc(p.vS.h, 1)
                p.vS.n += 1
                v.wait_ge(p.tS.h, t_base + 5 * h + 5)
                v.engine_nop().then_inc(p.vS.h, 1)
                p.vS.n += 1
                for dvc in range(4):
                    i = 4 * h + dvc
                    v.scalar_tensor_tensor(p.psB[:, 0:TC], opl[:, i * TC:(i + 1) * TC],
                                           rwc[:, dvc:dvc + 1], p.psD[:, 0:TC],
                                           ALU.mult, ALU.mult)
                    dst = qpl if i < 8 else kpl
                    v.tensor_mul(dst[:, (i % 8) * TC:(i % 8 + 1) * TC],
                                 p.psB[:, 0:TC], gsc[:, i * TC:(i + 1) * TC])\
                        .then_inc(p.vS.h, 1)
                    p.vS.n += 1

    with nc.Block() as blk:
        t_base = p.tS.n
        s_base = p.sS.n
        qw_base = [p.qW[i].n for i in range(4)]

        @blk.sync
        def _(sync):
            for m in range(8):
                for kc in range(16):
                    j = m * 16 + kc
                    slot = j % 4
                    if j >= 4:
                        sync.wait_ge(p.tS.h, t_base + (j - 3))
                    sync.dma_start(wsl[:, slot * 128:(slot + 1) * 128],
                                   p.wscr["L1_attn_Wo"][kc * 128:(kc + 1) * 128,
                                                        m * 128:(m + 1) * 128])\
                        .then_inc(p.qW[slot].h, 16)
                    p.qW[slot].n += 16

        @blk.tensor
        def _(t):
            for m in range(8):
                if m >= 2:
                    t.wait_ge(p.sS.h, s_base + (m - 1))
                for kc in range(16):
                    j = m * 16 + kc
                    slot = j % 4
                    t.wait_ge(p.qW[slot].h, qw_base[slot] + 16 * (j // 4 + 1))
                    srcpl = qpl if kc < 8 else kpl
                    t.matmul(p.psA[m % 2][:, 0:TC],
                             wsl[:, slot * 128:(slot + 1) * 128],
                             srcpl[:, (kc % 8) * TC:(kc % 8 + 1) * TC],
                             start=(kc == 0), stop=(kc == 15)).then_inc(p.tS.h, 1)
                    p.tS.n += 1

        @blk.scalar
        def _(s_):
            for m in range(8):
                s_.wait_ge(p.tS.h, t_base + 16 * (m + 1))
                s_.copy(aol[:, m * TC:(m + 1) * TC], p.psA[m % 2][:, 0:TC])\
                    .then_inc(p.sS.h, 1)
                p.sS.n += 1


def build_l1_ffn(p):
    nc = p.nc
    with contextlib.ExitStack() as es:
        xn4 = es.enter_context(nc.sbuf_tensor("f1_xn", [128, 8 * T], BF16))
        hq = es.enter_context(nc.sbuf_tensor("f1_hq", [128, 24 * TCW], BF16))
        sgt = es.enter_context(nc.sbuf_tensor("f1_sg", [128, 2 * TCW], F32))
        rg = es.enter_context(nc.sbuf_tensor("f1_rg", [128, 2 * 1024], BF16))
        ru = es.enter_context(nc.sbuf_tensor("f1_ru", [128, 2 * 1024], BF16))
        rd = es.enter_context(nc.sbuf_tensor("f1_rd", [128, 4 * 1024], BF16))
        op = es.enter_context(nc.sbuf_tensor("f1_op", [128, 8 * TCW], F32))

        plain_preln_plane(p, xn4)

        def psout(m):
            if m < 4:
                return p.psB[:, m * 512:m * 512 + TCW]
            return [p.psA[0], p.psA[1], p.psC, p.psD][m - 4][:, 0:TCW]

        for tc in range(T // TCW):
            with nc.Block() as blk:
                t_base = p.tS.n
                s_base = p.sS.n
                v_base = p.vS.n
                qx0 = [p.qX[i].n for i in range(2)]
                qz0 = [p.qZ[i].n for i in range(2)]
                qw0 = [p.qW[i].n for i in range(4)]

                @blk.sync
                def _(sync):
                    for s in range(24):
                        if s >= 2:
                            sync.wait_ge(p.tS.h, t_base + 16 * (s - 2) + 8)
                        sync.dma_start(rg[:, (s % 2) * 1024:(s % 2) * 1024 + 1024],
                                       p.wg_t[s * 128:(s + 1) * 128, :])\
                            .then_inc(p.qX[s % 2].h, 16)
                        p.qX[s % 2].n += 16
                        if s >= 2:
                            sync.wait_ge(p.tS.h, t_base + 16 * (s - 2) + 16)
                        sync.dma_start(ru[:, (s % 2) * 1024:(s % 2) * 1024 + 1024],
                                       p.wu_t[s * 128:(s + 1) * 128, :])\
                            .then_inc(p.qZ[s % 2].h, 16)
                        p.qZ[s % 2].n += 16
                    for s in range(24):
                        if s >= 4:
                            sync.wait_ge(p.tS.h, t_base + 384 + 8 * (s - 3))
                        sync.dma_start(rd[:, (s % 4) * 1024:(s % 4) * 1024 + 1024],
                                       p.wd_t[s * 128:(s + 1) * 128, :])\
                            .then_inc(p.qW[s % 4].h, 16)
                        p.qW[s % 4].n += 16

                @blk.tensor
                def _(t):
                    for s in range(24):
                        t.wait_ge(p.qX[s % 2].h, qx0[s % 2] + 16 * (s // 2 + 1))
                        if s >= 1:
                            t.wait_ge(p.sS.h, s_base + s)
                        for kc in range(8):
                            t.matmul(p.psA[0][:, 0:TCW],
                                     rg[:, (s % 2) * 1024 + kc * 128:
                                        (s % 2) * 1024 + (kc + 1) * 128],
                                     xn4[:, kc * T + tc * TCW: kc * T + tc * TCW + TCW],
                                     start=(kc == 0), stop=(kc == 7)).then_inc(p.tS.h, 1)
                            p.tS.n += 1
                        t.wait_ge(p.qZ[s % 2].h, qz0[s % 2] + 16 * (s // 2 + 1))
                        if s >= 1:
                            t.wait_ge(p.vS.h, v_base + s)
                        for kc in range(8):
                            t.matmul(p.psA[1][:, 0:TCW],
                                     ru[:, (s % 2) * 1024 + kc * 128:
                                        (s % 2) * 1024 + (kc + 1) * 128],
                                     xn4[:, kc * T + tc * TCW: kc * T + tc * TCW + TCW],
                                     start=(kc == 0), stop=(kc == 7)).then_inc(p.tS.h, 1)
                            p.tS.n += 1
                    t.wait_ge(p.sS.h, s_base + 24)
                    t.wait_ge(p.vS.h, v_base + 24)
                    for s in range(24):
                        t.wait_ge(p.qW[s % 4].h, qw0[s % 4] + 16 * (s // 4 + 1))
                        for m in range(8):
                            t.matmul(psout(m),
                                     rd[:, (s % 4) * 1024 + m * 128:
                                        (s % 4) * 1024 + (m + 1) * 128],
                                     hq[:, s * TCW:(s + 1) * TCW],
                                     start=(s == 0), stop=(s == 23)).then_inc(p.tS.h, 1)
                            p.tS.n += 1

                @blk.scalar
                def _(s_):
                    for s in range(24):
                        s_.wait_ge(p.tS.h, t_base + 16 * s + 8)
                        if s >= 2:
                            s_.wait_ge(p.vS.h, v_base + s - 1)
                        s_.activation(sgt[:, (s % 2) * TCW:(s % 2) * TCW + TCW],
                                      p.psA[0][:, 0:TCW], AF.Silu,
                                      scale=FC(1.0 / 127.0)).then_inc(p.sS.h, 1)
                        p.sS.n += 1
                    s_.wait_ge(p.tS.h, t_base + 576)
                    for m in range(8):
                        s_.activation(op[:, m * TCW:(m + 1) * TCW], psout(m),
                                      AF.Copy, scale=FC(1.0 / 16129.0)).then_inc(p.sS.h, 1)
                        p.sS.n += 1

                @blk.vector
                def _(v):
                    for s in range(24):
                        v.wait_ge(p.sS.h, s_base + s + 1)
                        v.wait_ge(p.tS.h, t_base + 16 * s + 16)
                        v.tensor_mul(hq[:, s * TCW:(s + 1) * TCW],
                                     sgt[:, (s % 2) * TCW:(s % 2) * TCW + TCW],
                                     p.psA[1][:, 0:TCW]).then_inc(p.vS.h, 1)
                        p.vS.n += 1

            post_ln_residual_tc(p, tc, lambda dc: op[:, dc * TCW:(dc + 1) * TCW])


def build_nc():
    nc = bass.Bass("TRN2", target_bir_lowering=False)
    p = P()
    p.nc = nc
    es = contextlib.ExitStack()
    p.es = es

    p.xin = nc.dram_tensor("x", [T, D], F32, kind="ExternalInput").ap()
    p.yout = nc.dram_tensor("y", [T, D], F32, kind="ExternalOutput").ap()
    p.win = {}
    wspecs = {
        "L0_attn_ln_g": [D], "L0_attn_ln_b": [D],
        "L0_attn_f_in_w": [2 * DI, D], "L0_attn_f_conv_w": [DI, 4], "L0_attn_f_conv_b": [DI],
        "L0_attn_f_xproj_w": [96, DI], "L0_attn_f_dt_w": [DI, DTRANK], "L0_attn_f_dt_b": [DI],
        "L0_attn_f_A_log": [DI, NSTATE], "L0_attn_f_D_skip": [DI], "L0_attn_f_out_w": [D, DI],
        "L0_attn_b_in_w": [2 * DI, D], "L0_attn_b_conv_w": [DI, 4], "L0_attn_b_conv_b": [DI],
        "L0_attn_b_xproj_w": [96, DI], "L0_attn_b_dt_w": [DI, DTRANK], "L0_attn_b_dt_b": [DI],
        "L0_attn_b_A_log": [DI, NSTATE], "L0_attn_b_D_skip": [DI], "L0_attn_b_out_w": [D, DI],
        "L0_ffn_w1": [D, 4096], "L0_ffn_b1": [4096], "L0_ffn_w2": [4096, D], "L0_ffn_b2": [D],
        "L1_attn_Wq": [D, 1024], "L1_attn_Wk": [D, 1024], "L1_attn_Wv": [D, 2048],
        "L1_attn_Wg": [D, 2048], "L1_attn_Wgk1": [D, 16], "L1_attn_Wgk2": [16, 1024],
        "L1_attn_bgk2": [1024], "L1_attn_Wo": [2048, D], "L1_attn_rms_w": [512],
    }
    for k, shp in wspecs.items():
        p.win[k] = nc.dram_tensor(k, shp, F32, kind="ExternalInput").ap()
    for k, shp in {"L1_ffn_w_gate": [3072, D], "L1_ffn_w_up": [3072, D],
                   "L1_ffn_w_down": [D, 3072]}.items():
        p.win[k] = nc.dram_tensor(k, shp, I32, kind="ExternalInput").ap()
    p.cin = {}
    for k, v in _consts_inputs().items():
        dt = F32 if v.dtype == np.float32 else BF16
        p.cin[k] = nc.dram_tensor(k, list(v.shape), dt, kind="ExternalInput").ap()

    sb = nc.alloc_sbuf_tensor
    p.r = sb("r", [128, 8 * T], F32)
    p.identf_t = sb("identf_t", [128, 128], F32)
    p.identb_t = sb("identb_t", [128, 128], BF16)
    p.maskut_t = sb("maskut_t", [128, 64], F32)
    p.ones_col = sb("ones_col", [128, 1], F32)
    p.ones_row = sb("ones_row", [1, 128], F32)
    p.row_m = sb("row_m", [1, 512], F32)
    p.row_rs = sb("row_rs", [1, 512], F32)
    p.row_v = sb("row_v", [1, 512], F32)
    p.row_a = sb("row_a", [1, 512], F32)
    p.row_b = sb("row_b", [1, 512], F32)
    p.sqtile = sb("sqtile", [128, 512], F32)
    p.vtmp = sb("vtmp", [128, 512], F32)
    p.zeros = sb("zeros", [128, 512], F32)
    _pv = _float_pool_values()
    p.cpool = sb("cpool", [128, len(_pv)], F32)
    for i, v in enumerate(_pv):
        nc.const_aps.register(F32, v, p.cpool[:, i:i + 1])

    pt = nc.alloc_psum_tensor
    p.psA = [pt("psA0", [128, 512], F32), pt("psA1", [128, 512], F32)]
    p.psB = pt("psB", [128, 2048], F32)
    p.psC = pt("psC", [128, 512], F32)
    p.psD = pt("psD", [128, 512], F32)
    p.pst = [p.psA[0], p.psA[1]]
    p.psrow = [p.psC, p.psD]
    p.psbc = [p.psA[0], p.psA[1]]

    p.qS = Ctr(es, nc, "qS")
    p.qG = Ctr(es, nc, "qG")
    p.qH = Ctr(es, nc, "qH")
    p.qW = [Ctr(es, nc, "qW%d" % i) for i in range(4)]
    p.qX = [Ctr(es, nc, "qX%d" % i) for i in range(2)]
    p.qZ = [Ctr(es, nc, "qZ%d" % i) for i in range(2)]
    p.qY = [Ctr(es, nc, "qY%d" % i) for i in range(2)]
    p.tS = Ctr(es, nc, "tS")
    p.sS = Ctr(es, nc, "sS")
    p.vS = Ctr(es, nc, "vS")
    p.gS = Ctr(es, nc, "gS")

    with nc.Block() as blk:
        @blk.sync
        def _(sync):
            for name, dst in [("c_identf", p.identf_t), ("c_identb", p.identb_t),
                              ("c_maskut", p.maskut_t), ("c_pool", p.cpool)]:
                sync.dma_start(dst[:], p.cin[name]).then_inc(p.qS.h, 16)
                p.qS.n += 16
            sync.wait_ge(p.qS.h, p.qS.n)

        @blk.vector
        def _(v):
            v.memset(p.ones_col[:], 1.0)
            v.memset(p.ones_row[:], 1.0)
            v.memset(p.zeros[:], 0.0)

    prep_weights(p)
    input_transpose(p)
    load_cols(p)

    if EN_L0A:
        build_l0_attn(p)
    if EN_L0F:
        build_l0_ffn(p)
    if EN_L1A:
        build_l1_attn(p)
    if EN_L1F:
        build_l1_ffn(p)

    output_transpose(p)
    es.close()
    return nc


_consts_cache = _consts_inputs()


def kernel(**inputs):
    x = np.ascontiguousarray(np.asarray(inputs["L0_attn_x"], dtype=np.float32))
    nc = build_nc()
    in_maps = []
    for b in range(8):
        m = {"x": x[b]}
        for k, v in inputs.items():
            if k == "L0_attn_x":
                continue
            m[k] = np.ascontiguousarray(np.asarray(v))
        for k, v in _consts_cache.items():
            m[k] = v
        in_maps.append(m)
    res = run_bass_kernel_spmd(nc, in_maps, core_ids=list(range(8)))
    return np.stack([np.asarray(r["y"], dtype=np.float32) for r in res.results])


# version 05
# speedup vs baseline: 1.0486x; 1.0486x over previous
import math
import contextlib
import numpy as np
import ml_dtypes
import concourse.bass as bass
import concourse.mybir as mybir
from concourse.bass_utils import run_bass_kernel_spmd

F32 = mybir.dt.float32
BF16 = mybir.dt.bfloat16
I32 = mybir.dt.int32
U8 = mybir.dt.uint8
AF = mybir.ActivationFunctionType
ALU = mybir.AluOpType

T = 2048
D = 1024
DI = 2048
NSTATE = 16
DTRANK = 64
EPS = 1e-5
TCW = 512

EN_L0A = True
EN_L0F = True
EN_L1A = True
EN_L1F = True
DEBUG_FFN = False
DEBUG_GLA = False


def _gelu_consts():
    X = np.linspace(-4.0, 4.0, 33)
    g2 = 0.5 * X * np.vectorize(math.erf)(X / np.sqrt(2.0))
    Ey = g2[16:]
    m = (Ey[1:] - Ey[:-1]) / 0.25
    return float(np.float32(m[0])), [float(np.float32(v)) for v in (m[1:] - m[:-1])]


GELU_M0, GELU_DM = _gelu_consts()


def _float_pool_values():
    vals = [EPS, GELU_M0, 1.0 / 127.0, 1.0 / 16129.0, math.log(1.0 / 16.0)]
    for i in range(1, 16):
        ad = abs(GELU_DM[i - 1])
        vals.append(ad)
        vals.append(-ad * 64.0 * i)
    out = []
    for v in vals:
        out.append(float(np.float32(v)))
    return sorted(set(out))


def _consts_inputs():
    identf = np.eye(128, dtype=np.float32)
    identb = np.eye(128, dtype=ml_dtypes.bfloat16)
    mask_ut = np.tile(np.triu(np.ones((64, 64), dtype=np.float32)), (2, 1))
    pool = np.tile(np.array(_float_pool_values(), dtype=np.float32).reshape(1, -1),
                   (128, 1))
    return {"c_identf": identf, "c_identb": identb, "c_maskut": mask_ut,
            "c_pool": pool}


def FC(v):
    return float(np.float32(v))


class Ctr:
    def __init__(self, es, nc, name):
        self.h = es.enter_context(nc.semaphore(name))
        self.n = 0


class P:
    pass


def load_cols(p):
    nc = p.nc
    vecs = [("L0_attn_ln_g", D), ("L0_attn_ln_b", D),
            ("L0_attn_f_conv_b", DI), ("L0_attn_b_conv_b", DI),
            ("L0_attn_f_dt_b", DI), ("L0_attn_b_dt_b", DI),
            ("L0_attn_f_D_skip", DI), ("L0_attn_b_D_skip", DI),
            ("L0_ffn_b1", 4096), ("L0_ffn_b2", D),
            ("L1_attn_bgk2", 1024), ("L1_attn_rms_w", 512)]
    sb = nc.alloc_sbuf_tensor
    p.cols = {}
    for name, ln in vecs:
        p.cols[name] = sb("col_" + name, [128, ln // 128], F32)
    p.Acol = {"f": sb("Acol_f", [128, NSTATE * 16], F32),
              "b": sb("Acol_b", [128, NSTATE * 16], F32)}
    p.convw = {"f": sb("convw_f", [128, 64], F32),
               "b": sb("convw_b", [128, 64], F32)}
    p.b1c256 = sb("b1c256", [128, 32], F32)
    p.b2c256 = sb("b2c256", [128, 8], F32)

    with nc.sbuf_tensor("vec_stage", [128, 128], F32) as stage:
        for name, ln in vecs:
            a = ln // 128
            with nc.Block() as blk:
                @blk.sync
                def _(sync):
                    sync.dma_start(stage[0:a, 0:128],
                                   p.win[name].rearrange("(a b) -> a b", b=128))\
                        .then_inc(p.qS.h, 16)
                    p.qS.n += 16
                    sync.wait_ge(p.qS.h, p.qS.n)
            with nc.Block() as blk:
                @blk.tensor
                def _(t):
                    t.transpose(p.pst[0][:, 0:128], stage[:, 0:128], p.identf_t[:])\
                        .then_inc(p.tS.h, 1)
                    p.tS.n += 1

                @blk.scalar
                def _(s):
                    s.wait_ge(p.tS.h, p.tS.n)
                    s.copy(p.cols[name][:, 0:a], p.pst[0][:, 0:a]).then_inc(p.sS.h, 1)
                    p.sS.n += 1

    with nc.Block() as blk:
        @blk.sync
        def _(sync):
            for d, key in [("f", "L0_attn_f_A_log"), ("b", "L0_attn_b_A_log")]:
                for ch in range(16):
                    sync.dma_start(p.Acol[d][:, ch * 16:(ch + 1) * 16],
                                   p.win[key][ch * 128:(ch + 1) * 128, :])\
                        .then_inc(p.qS.h, 16)
                    p.qS.n += 16
            for d, key in [("f", "L0_attn_f_conv_w"), ("b", "L0_attn_b_conv_w")]:
                for ch in range(16):
                    sync.dma_start(p.convw[d][:, ch * 4:(ch + 1) * 4],
                                   p.win[key][ch * 128:(ch + 1) * 128, :])\
                        .then_inc(p.qS.h, 16)
                    p.qS.n += 16
            sync.wait_ge(p.qS.h, p.qS.n)

    with nc.Block() as blk:
        @blk.scalar
        def _(s):
            for d in ["f", "b"]:
                s.activation(p.Acol[d][:], p.Acol[d][:], AF.Exp)
            s.mul(p.b1c256[:], p.cols["L0_ffn_b1"][:], 256.0)
            s.mul(p.b2c256[:], p.cols["L0_ffn_b2"][:], 256.0)

    with nc.Block() as blk:
        @blk.vector
        def _(v):
            for d in ["f", "b"]:
                v.tensor_scalar(p.Acol[d][:], p.Acol[d][:], -1.0, None, ALU.mult)


def prep_weights_gpsimd(p, g, wbounce):
    names = p.prep_names
    jobs = []
    for name in names:
        R, C = p.win[name].shape
        for r0 in range(0, R, 128):
            jobs.append((name, r0, min(r0 + 128, R)))
    qg0 = p.qG.n
    qh0 = p.qH.n
    for i, (name, r0, r1) in enumerate(jobs):
        C = p.win[name].shape[1]
        buf = wbounce[0:r1 - r0, (i % 2) * 4096:(i % 2) * 4096 + C]
        if i >= 2:
            g.wait_ge(p.qH.h, qh0 + 16 * (i - 1))
        g.dma_start(buf, p.win[name][r0:r1, :]).then_inc(p.qG.h, 16)
        p.qG.n += 16
        g.wait_ge(p.qG.h, p.qG.n)
        g.dma_start(p.wscr[name][r0:r1, :], buf).then_inc(p.qH.h, 16)
        p.qH.n += 16
    g.wait_ge(p.qH.h, p.qH.n)


def prep_weights(p):
    nc = p.nc
    names = ["L0_attn_f_in_w", "L0_attn_b_in_w", "L0_attn_f_out_w", "L0_attn_b_out_w",
             "L0_attn_f_xproj_w", "L0_attn_b_xproj_w", "L0_attn_f_dt_w", "L0_attn_b_dt_w",
             "L0_ffn_w1", "L0_ffn_w2",
             "L1_attn_Wq", "L1_attn_Wk", "L1_attn_Wv", "L1_attn_Wg",
             "L1_attn_Wgk1", "L1_attn_Wgk2", "L1_attn_Wo"]
    p.wscr = {}
    for name in names:
        R, C = p.win[name].shape
        p.wscr[name] = nc.dram_tensor("scr_" + name, [R, C], BF16, kind="Internal").ap()

    p.prep_names = names
    p.wg_t = nc.dram_tensor("scr3_gate", [24 * 128, 1024], BF16, kind="Internal").ap()
    p.wu_t = nc.dram_tensor("scr3_up", [24 * 128, 1024], BF16, kind="Internal").ap()
    p.wd_t = nc.dram_tensor("scr3_down", [24 * 128, 1024], BF16, kind="Internal").ap()


def input_transpose(p):
    nc = p.nc
    njobs = ([("g", s) for s in range(24)] + [("u", s) for s in range(24)]
             + [("d", m) for m in range(8)])
    ntr = [24 if k == "d" else 8 for k, _ in njobs]
    cumT = []
    acc = 0
    for n in ntr:
        acc += n
        cumT.append(acc)
    NJ = len(njobs)
    with nc.sbuf_tensor("xtok_in", [128, 16 * D], F32) as xtok, \
         nc.sbuf_tensor("wbounce", [128, 2 * 4096], BF16) as wbounce, \
         nc.sbuf_tensor("obuf", [128, 2 * 1024], BF16) as obuf, \
         nc.sbuf_tensor("obufd", [128, 4 * 128], BF16) as obufd:
        with nc.Block() as blk:
            q0 = p.qS.n
            qg0 = p.qG.n
            qx0 = [p.qX[i].n for i in range(2)]
            qw0 = [p.qW[i].n for i in range(4)]
            t_base = p.tS.n
            s_base = p.sS.n
            tiles = [(tt, dc) for tt in range(T // 128) for dc in range(8)]
            NX = len(tiles)

            @blk.sync
            def _(sync):
                for tt in range(T // 128):
                    sync.dma_start(xtok[:, tt * D:(tt + 1) * D],
                                   p.xin[tt * 128:(tt + 1) * 128, :]).then_inc(p.qS.h, 16)
                    p.qS.n += 16

            @blk.gpsimd
            def _(g):
                srcs = {"g": p.win["L1_ffn_w_gate"], "u": p.win["L1_ffn_w_up"],
                        "d": p.win["L1_ffn_w_down"]}
                for j, (kind, idx) in enumerate(njobs):
                    ncols = 3072 if kind == "d" else 1024
                    buf = wbounce[0:128, (j % 2) * 4096:(j % 2) * 4096 + ncols]
                    if j >= 2:
                        g.wait_ge(p.tS.h, t_base + NX + cumT[j - 2])
                    g.dma_start(buf, srcs[kind][idx * 128:(idx + 1) * 128, :])\
                        .then_inc(p.qG.h, 16)
                    p.qG.n += 16
                g.wait_ge(p.tS.h, t_base + NX + cumT[NJ - 1])
                prep_weights_gpsimd(p, g, wbounce)

            @blk.tensor
            def _(t):
                t.wait_ge(p.qS.h, q0 + 16 * (T // 128))
                for i, (tt, dc) in enumerate(tiles):
                    if i >= 2:
                        t.wait_ge(p.sS.h, s_base + i - 1)
                    t.transpose(p.pst[i % 2][:, 0:128],
                                xtok[:, tt * D + dc * 128: tt * D + dc * 128 + 128],
                                p.identf_t[:]).then_inc(p.tS.h, 1)
                    p.tS.n += 1
                gi = NX
                for j, (kind, idx) in enumerate(njobs):
                    t.wait_ge(p.qG.h, qg0 + 16 * (j + 1))
                    buf = wbounce[0:128, (j % 2) * 4096:]
                    for i in range(ntr[j]):
                        if gi >= 2:
                            t.wait_ge(p.sS.h, s_base + gi - 1)
                        t.transpose(p.pst[gi % 2].bitcast(BF16)[:, 0:128],
                                    buf[:, i * 128:(i + 1) * 128],
                                    p.identb_t[:]).then_inc(p.tS.h, 1)
                        p.tS.n += 1
                        gi += 1

            @blk.scalar
            def _(s):
                for i, (tt, dc) in enumerate(tiles):
                    s.wait_ge(p.tS.h, t_base + i + 1)
                    s.copy(p.r[:, dc * T + tt * 128: dc * T + tt * 128 + 128],
                           p.pst[i % 2][:, 0:128]).then_inc(p.sS.h, 1)
                    p.sS.n += 1
                gi = NX
                ti = 0
                for j, (kind, idx) in enumerate(njobs):
                    if kind in ("g", "u"):
                        dstD = p.wg_t if kind == "g" else p.wu_t
                        if j >= 2:
                            s.wait_ge(p.qX[j % 2].h, qx0[j % 2] + 16 * (j // 2))
                        oslot = obuf[:, (j % 2) * 1024:(j % 2) * 1024 + 1024]
                        for i in range(8):
                            s.wait_ge(p.tS.h, t_base + gi + 1)
                            s.copy(oslot[:, i * 128:(i + 1) * 128],
                                   p.pst[gi % 2].bitcast(BF16)[:, 0:128])\
                                .then_inc(p.sS.h, 1)
                            p.sS.n += 1
                            gi += 1
                        s.dma_start(dstD[idx * 128:(idx + 1) * 128, :], oslot)\
                            .then_inc(p.qX[j % 2].h, 16)
                        p.qX[j % 2].n += 16
                    else:
                        for i in range(24):
                            s.wait_ge(p.tS.h, t_base + gi + 1)
                            if ti >= 4:
                                s.wait_ge(p.qW[ti % 4].h, qw0[ti % 4] + 16 * (ti // 4))
                            dslot = obufd[:, (ti % 4) * 128:(ti % 4) * 128 + 128]
                            s.copy(dslot, p.pst[gi % 2].bitcast(BF16)[:, 0:128])\
                                .then_inc(p.sS.h, 1)
                            p.sS.n += 1
                            s.dma_start(p.wd_t[i * 128:(i + 1) * 128,
                                               idx * 128:(idx + 1) * 128], dslot)\
                                .then_inc(p.qW[ti % 4].h, 16)
                            p.qW[ti % 4].n += 16
                            gi += 1
                            ti += 1
                for k in range(2):
                    s.wait_ge(p.qX[k].h, p.qX[k].n)
                for k in range(4):
                    s.wait_ge(p.qW[k].h, p.qW[k].n)


def output_transpose(p):
    nc = p.nc
    with nc.sbuf_tensor("xtok_out", [128, 16 * D], F32) as xtok:
        with nc.Block() as blk:
            tiles = [(tt, dc) for tt in range(T // 128) for dc in range(8)]
            t_base = p.tS.n
            s_base = p.sS.n

            @blk.tensor
            def _(t):
                for i, (tt, dc) in enumerate(tiles):
                    if i >= 2:
                        t.wait_ge(p.sS.h, s_base + i - 1)
                    t.transpose(p.pst[i % 2][:, 0:128],
                                p.r[:, dc * T + tt * 128: dc * T + tt * 128 + 128],
                                p.identf_t[:]).then_inc(p.tS.h, 1)
                    p.tS.n += 1

            @blk.scalar
            def _(s):
                for i, (tt, dc) in enumerate(tiles):
                    s.wait_ge(p.tS.h, t_base + i + 1)
                    s.copy(xtok[:, tt * D + dc * 128: tt * D + dc * 128 + 128],
                           p.pst[i % 2][:, 0:128]).then_inc(p.sS.h, 1)
                    p.sS.n += 1

        with nc.Block() as blk:
            @blk.sync
            def _(sync):
                sync.wait_ge(p.sS.h, p.sS.n)
                for tt in range(T // 128):
                    sync.dma_start(p.yout[tt * 128:(tt + 1) * 128, :],
                                   xtok[:, tt * D:(tt + 1) * D]).then_inc(p.qS.h, 16)
                    p.qS.n += 16
                sync.wait_ge(p.qS.h, p.qS.n)


def ln_stats_tc(p, blk, tile_of_dc, tcw, nchunk=8):
    inv_n = 1.0 / (nchunk * 128)
    t_base = p.tS.n
    s_base = p.sS.n
    v_base = p.vS.n

    @blk.scalar
    def _(s):
        for dc in range(nchunk):
            if dc >= 2:
                s.wait_ge(p.tS.h, t_base + 2 * dc - 2)
            sq = p.sqtile if dc % 2 == 0 else p.vtmp
            s.activation(sq[:, 0:tcw], tile_of_dc(dc), AF.Square).then_inc(p.sS.h, 1)
            p.sS.n += 1

    @blk.tensor
    def _(t):
        for dc in range(nchunk):
            t.matmul(p.psrow[0][0:1, 0:tcw], p.ones_col[:, 0:1], tile_of_dc(dc),
                     start=(dc == 0), stop=(dc == nchunk - 1)).then_inc(p.tS.h, 1)
            p.tS.n += 1
            t.wait_ge(p.sS.h, s_base + dc + 1)
            sq = p.sqtile if dc % 2 == 0 else p.vtmp
            t.matmul(p.psrow[1][0:1, 0:tcw], p.ones_col[:, 0:1],
                     sq[:, 0:tcw],
                     start=(dc == 0), stop=(dc == nchunk - 1)).then_inc(p.tS.h, 1)
            p.tS.n += 1

    @blk.vector
    def _(v):
        v.wait_ge(p.tS.h, t_base + 2 * nchunk)
        sl = slice(0, tcw)
        v.tensor_scalar(p.row_m[0:1, sl], p.psrow[0][0:1, 0:tcw], inv_n, None, ALU.mult)
        v.tensor_scalar(p.row_a[0:1, sl], p.psrow[1][0:1, 0:tcw], inv_n, None, ALU.mult)
        v.tensor_mul(p.row_b[0:1, sl], p.row_m[0:1, sl], p.row_m[0:1, sl])
        v.tensor_sub(p.row_v[0:1, sl], p.row_a[0:1, sl], p.row_b[0:1, sl])\
            .then_inc(p.vS.h, 1)
        p.vS.n += 1

    @blk.scalar
    def _(s):
        s.wait_ge(p.vS.h, v_base + 1)
        sl = slice(0, tcw)
        s.activation(p.row_a[0:1, sl], p.row_v[0:1, sl], AF.Sqrt, bias=FC(EPS))\
            .then_inc(p.sS.h, 1)
        p.sS.n += 1

    @blk.vector
    def _(v):
        v.wait_ge(p.sS.h, p.sS.n)
        sl = slice(0, tcw)
        v.reciprocal(p.row_rs[0:1, sl], p.row_a[0:1, sl]).then_inc(p.vS.h, 1)
        p.vS.n += 1


def bcast_row(p, t, row_ap, bank, tcw):
    t.matmul(p.psbc[bank][:, 0:tcw], p.ones_row[0:1, :], row_ap,
             start=True, stop=True).then_inc(p.tS.h, 1)
    p.tS.n += 1


def plain_preln_plane(p, dst_bf16):
    nc = p.nc
    for tc in range(T // TCW):
        with nc.Block() as blk:
            ln_stats_tc(p, blk,
                        lambda dc: p.r[:, dc * T + tc * TCW: dc * T + tc * TCW + TCW],
                        TCW, 8)
        with nc.Block() as blk:
            t_base = p.tS.n

            @blk.tensor
            def _(t):
                bcast_row(p, t, p.row_m[0:1, 0:TCW], 0, TCW)
                bcast_row(p, t, p.row_rs[0:1, 0:TCW], 1, TCW)

            @blk.vector
            def _(v):
                v.wait_ge(p.tS.h, t_base + 2)
                for dc in range(8):
                    sl = slice(dc * T + tc * TCW, dc * T + tc * TCW + TCW)
                    v.tensor_sub(p.vtmp[:, 0:TCW], p.r[:, sl], p.psbc[0][:, 0:TCW])
                    v.tensor_mul(dst_bf16[:, sl], p.vtmp[:, 0:TCW], p.psbc[1][:, 0:TCW])\
                        .then_inc(p.vS.h, 1)
                    p.vS.n += 1


def post_ln_residual_tc(p, tc, tile_of_dc, tcw=TCW):
    nc = p.nc
    with nc.Block() as blk:
        ln_stats_tc(p, blk, tile_of_dc, tcw, 8)
    with nc.Block() as blk:
        t_base = p.tS.n

        @blk.tensor
        def _(t):
            bcast_row(p, t, p.row_m[0:1, 0:tcw], 0, tcw)
            bcast_row(p, t, p.row_rs[0:1, 0:tcw], 1, tcw)

        @blk.vector
        def _(v):
            v.wait_ge(p.tS.h, t_base + 2)
            for dc in range(8):
                sl = slice(dc * T + tc * tcw, dc * T + tc * tcw + tcw)
                v.tensor_sub(p.vtmp[:, 0:tcw], tile_of_dc(dc), p.psbc[0][:, 0:tcw])
                v.tensor_mul(p.vtmp[:, 0:tcw], p.vtmp[:, 0:tcw], p.psbc[1][:, 0:tcw])
                v.tensor_add(p.r[:, sl], p.r[:, sl], p.vtmp[:, 0:tcw])\
                    .then_inc(p.vS.h, 1)
                p.vS.n += 1


def build_l0_ffn(p):
    nc = p.nc
    if DEBUG_FFN:
        p.dbg_xn = nc.dram_tensor("dbg_xn", [128, 8 * T], BF16, kind="ExternalOutput").ap()
        p.dbg_hq = nc.dram_tensor("dbg_hq", [128, 32 * TCW], BF16, kind="ExternalOutput").ap()
        p.dbg_op = nc.dram_tensor("dbg_op", [128, 8 * TCW], F32, kind="ExternalOutput").ap()
    with contextlib.ExitStack() as es:
        xn = es.enter_context(nc.sbuf_tensor("ffn_xn", [128, 8 * T], BF16))
        hq = es.enter_context(nc.sbuf_tensor("ffn_hq", [128, 32 * TCW], BF16))
        wt = es.enter_context(nc.sbuf_tensor("ffn_wt", [128, 4 * 128], BF16))
        gk = es.enter_context(nc.sbuf_tensor("ffn_gk", [128, 4 * TCW], I32))
        ga = es.enter_context(nc.sbuf_tensor("ffn_ga", [128, TCW], F32))
        gr = es.enter_context(nc.sbuf_tensor("ffn_gr", [128, 2 * TCW], F32))
        gm = es.enter_context(nc.sbuf_tensor("ffn_gm", [128, TCW], U8))
        op = es.enter_context(nc.sbuf_tensor("ffn_op", [128, 8 * TCW], F32))

        plain_preln_plane(p, xn)

        w1s = p.wscr["L0_ffn_w1"]
        w2s = p.wscr["L0_ffn_w2"]
        for tc in range(T // TCW):
            with nc.Block() as blk:
                qw_base = [p.qW[i].n for i in range(4)]
                t_base = p.tS.n
                s_base = p.sS.n
                v_base = p.vS.n

                @blk.sync
                def _(sync):
                    for fs in range(32):
                        for kc in range(8):
                            j = fs * 8 + kc
                            slot = j % 4
                            if j >= 4:
                                sync.wait_ge(p.tS.h, t_base + (j - 3))
                            sync.dma_start(wt[:, slot * 128:(slot + 1) * 128],
                                           w1s[kc * 128:(kc + 1) * 128,
                                               fs * 128:(fs + 1) * 128])\
                                .then_inc(p.qW[slot].h, 16)
                            p.qW[slot].n += 16

                @blk.tensor
                def _(t):
                    for fs in range(32):
                        if fs >= 2:
                            t.wait_ge(p.sS.h, s_base + 17 * (fs - 2) + 2)
                        for kc in range(8):
                            j = fs * 8 + kc
                            slot = j % 4
                            t.wait_ge(p.qW[slot].h, qw_base[slot] + 16 * (j // 4 + 1))
                            t.matmul(p.psA[fs % 2][:, 0:TCW],
                                     wt[:, slot * 128:(slot + 1) * 128],
                                     xn[:, kc * T + tc * TCW: kc * T + tc * TCW + TCW],
                                     start=(kc == 0), stop=(kc == 7)).then_inc(p.tS.h, 1)
                            p.tS.n += 1

                @blk.scalar
                def _(s):
                    for fs in range(32):
                        gslot = (fs % 2) * 2 * TCW
                        if fs >= 1:
                            s.wait_ge(p.vS.h, v_base + 20 * fs)
                        s.wait_ge(p.tS.h, t_base + 8 * (fs + 1))
                        s.activation(gk[:, gslot:gslot + TCW], p.psA[fs % 2][:, 0:TCW],
                                     AF.Identity, bias=p.b1c256[:, fs:fs + 1], scale=256.0)\
                            .then_inc(p.sS.h, 1)
                        s.activation(gk[:, gslot + TCW:gslot + 2 * TCW],
                                     p.psA[fs % 2][:, 0:TCW],
                                     AF.Abs, bias=p.b1c256[:, fs:fs + 1], scale=256.0)\
                            .then_inc(p.sS.h, 1)
                        p.sS.n += 2
                        for i in range(1, 16):
                            ad = abs(GELU_DM[i - 1])
                            if i >= 3:
                                s.wait_ge(p.vS.h, v_base + 20 * fs + (i - 2) + 2)
                            s.activation(gr[:, (i % 2) * TCW:(i % 2) * TCW + TCW],
                                         gk[:, gslot + TCW:gslot + 2 * TCW], AF.Relu,
                                         bias=FC(-ad * 64.0 * i), scale=FC(ad))\
                                .then_inc(p.sS.h, 1)
                            p.sS.n += 1

                @blk.vector
                def _(v):
                    for fs in range(32):
                        gslot = (fs % 2) * 2 * TCW
                        base_s = s_base + fs * 17
                        v.wait_ge(p.sS.h, base_s + 2)
                        v.tensor_scalar(ga[:, 0:TCW], gk[:, gslot:gslot + TCW],
                                        0.5, None, ALU.mult).then_inc(p.vS.h, 1)
                        v.scalar_tensor_tensor(ga[:, 0:TCW],
                                               gk[:, gslot + TCW:gslot + 2 * TCW],
                                               GELU_M0, ga[:, 0:TCW],
                                               ALU.mult, ALU.add).then_inc(p.vS.h, 1)
                        p.vS.n += 2
                        for i in range(1, 16):
                            sgn = 1.0 if GELU_DM[i - 1] >= 0 else -1.0
                            v.wait_ge(p.sS.h, base_s + 2 + i)
                            v.scalar_tensor_tensor(
                                ga[:, 0:TCW],
                                gr[:, (i % 2) * TCW:(i % 2) * TCW + TCW],
                                sgn, ga[:, 0:TCW], ALU.mult, ALU.add)\
                                .then_inc(p.vS.h, 1)
                            p.vS.n += 1
                        v.tensor_single_scalar(gm[:, 0:TCW], gk[:, gslot:gslot + TCW],
                                               1024.0, ALU.is_gt)
                        v.copy_predicated(ga[:, 0:TCW], gm[:, 0:TCW],
                                          gk[:, gslot:gslot + TCW])
                        v.tensor_single_scalar(gm[:, 0:TCW], gk[:, gslot:gslot + TCW],
                                               -1024.0, ALU.is_lt)
                        v.copy_predicated(ga[:, 0:TCW], gm[:, 0:TCW], p.zeros[:, 0:TCW])
                        v.tensor_scalar(gk[:, gslot:gslot + TCW], ga[:, 0:TCW],
                                        1.0, None, ALU.mult)
                        v.tensor_scalar(hq[:, fs * TCW:(fs + 1) * TCW],
                                        gk[:, gslot:gslot + TCW],
                                        1.0 / 256.0, None, ALU.mult).then_inc(p.vS.h, 3)
                        p.vS.n += 3

            with nc.Block() as blk:
                qw_base = [p.qW[i].n for i in range(4)]
                t_base = p.tS.n
                s_base = p.sS.n
                v_base = p.vS.n

                @blk.sync
                def _(sync):
                    for ds in range(8):
                        for kc in range(32):
                            j = ds * 32 + kc
                            slot = j % 4
                            if j >= 4:
                                sync.wait_ge(p.tS.h, t_base + (j - 3))
                            sync.dma_start(wt[:, slot * 128:(slot + 1) * 128],
                                           w2s[kc * 128:(kc + 1) * 128,
                                               ds * 128:(ds + 1) * 128])\
                                .then_inc(p.qW[slot].h, 16)
                            p.qW[slot].n += 16

                @blk.tensor
                def _(t):
                    for ds in range(8):
                        if ds >= 2:
                            t.wait_ge(p.sS.h, s_base + (ds - 1))
                        for kc in range(32):
                            j = ds * 32 + kc
                            slot = j % 4
                            t.wait_ge(p.qW[slot].h, qw_base[slot] + 16 * (j // 4 + 1))
                            t.matmul(p.psA[ds % 2][:, 0:TCW],
                                     wt[:, slot * 128:(slot + 1) * 128],
                                     hq[:, kc * TCW:(kc + 1) * TCW],
                                     start=(kc == 0), stop=(kc == 31)).then_inc(p.tS.h, 1)
                            p.tS.n += 1

                @blk.scalar
                def _(s):
                    for ds in range(8):
                        if ds >= 2:
                            s.wait_ge(p.vS.h, v_base + ds - 1)
                        s.wait_ge(p.tS.h, t_base + 32 * (ds + 1))
                        s.activation(gk[:, (ds % 2) * TCW:(ds % 2) * TCW + TCW],
                                     p.psA[ds % 2][:, 0:TCW], AF.Identity,
                                     bias=p.b2c256[:, ds:ds + 1], scale=256.0)\
                            .then_inc(p.sS.h, 1)
                        p.sS.n += 1

                @blk.vector
                def _(v):
                    for ds in range(8):
                        v.wait_ge(p.sS.h, s_base + ds + 1)
                        v.tensor_scalar(op[:, ds * TCW:(ds + 1) * TCW],
                                        gk[:, (ds % 2) * TCW:(ds % 2) * TCW + TCW],
                                        1.0 / 256.0, None, ALU.mult).then_inc(p.vS.h, 1)
                        p.vS.n += 1

            if DEBUG_FFN and tc == 0:
                with nc.Block() as blk:
                    @blk.sync
                    def _(sync):
                        sync.dma_start(p.dbg_xn, xn[:]).then_inc(p.qS.h, 16)
                        sync.dma_start(p.dbg_hq, hq[:]).then_inc(p.qS.h, 16)
                        sync.dma_start(p.dbg_op, op[:]).then_inc(p.qS.h, 16)
                        p.qS.n += 48
                        sync.wait_ge(p.qS.h, p.qS.n)
            post_ln_residual_tc(p, tc, lambda dc: op[:, dc * TCW:(dc + 1) * TCW])


def build_l0_attn(p):
    nc = p.nc
    NTC = 4
    W = 512

    scr_xc = {d: nc.dram_tensor("scr_xc_" + d, [DI, T], F32, kind="Internal").ap()
              for d in "fb"}
    scr_sz = {d: nc.dram_tensor("scr_sz_" + d, [DI, T], F32, kind="Internal").ap()
              for d in "fb"}
    scr_y2 = {d: nc.dram_tensor("scr_y2_" + d, [DI, T], BF16, kind="Internal").ap()
              for d in "fb"}
    scr_yf = nc.dram_tensor("scr_yf", [D, T], F32, kind="Internal").ap()

    with contextlib.ExitStack() as es0:
        dbltmp = {d: es0.enter_context(nc.sbuf_tensor("dbltmp_" + d, [96, T], F32))
                  for d in "fb"}
        dtvec = {d: es0.enter_context(nc.sbuf_tensor("dtvec_" + d, [64, T], BF16))
                 for d in "fb"}

        state = es0.enter_context(nc.sbuf_tensor("mb_state", [128, 256], F32))

        with contextlib.ExitStack() as es1:
            xn = es1.enter_context(nc.sbuf_tensor("mb_xn", [128, 8 * T], BF16))
            gcol = p.cols["L0_attn_ln_g"]
            bcol = p.cols["L0_attn_ln_b"]
            for tc in range(NTC):
                with nc.Block() as blk:
                    ln_stats_tc(p, blk,
                                lambda dc: p.r[:, dc * T + tc * W: dc * T + tc * W + W],
                                W, 8)
                with nc.Block() as blk:
                    v_base = p.vS.n
                    s_base = p.sS.n

                    @blk.vector
                    def _(v):
                        v.tensor_scalar(p.row_a[0:1, 0:W], p.row_v[0:1, 0:W], FC(EPS),
                                        None, ALU.add)
                        v.reciprocal(p.row_b[0:1, 0:W], p.row_a[0:1, 0:W])
                        v.tensor_mul(p.row_a[0:1, 0:W], p.row_v[0:1, 0:W],
                                     p.row_b[0:1, 0:W]).then_inc(p.vS.h, 1)
                        p.vS.n += 1

                    @blk.scalar
                    def _(s):
                        s.wait_ge(p.vS.h, v_base + 1)
                        s.activation(p.row_b[0:1, 0:W], p.row_a[0:1, 0:W], AF.Sqrt,
                                     bias=FC(EPS)).then_inc(p.sS.h, 1)
                        p.sS.n += 1

                    @blk.vector
                    def _(v):
                        v.wait_ge(p.sS.h, s_base + 1)
                        v.reciprocal(p.row_a[0:1, 0:W], p.row_b[0:1, 0:W])
                        v.tensor_mul(p.row_rs[0:1, 0:W], p.row_rs[0:1, 0:W],
                                     p.row_a[0:1, 0:W]).then_inc(p.vS.h, 1)
                        p.vS.n += 1

                with nc.Block() as blk:
                    t_base = p.tS.n

                    @blk.tensor
                    def _(t):
                        bcast_row(p, t, p.row_m[0:1, 0:W], 0, W)
                        bcast_row(p, t, p.row_rs[0:1, 0:W], 1, W)

                    @blk.vector
                    def _(v):
                        v.wait_ge(p.tS.h, t_base + 2)
                        for dc in range(8):
                            sl = slice(dc * T + tc * W, dc * T + tc * W + W)
                            v.tensor_sub(p.vtmp[:, 0:W], p.r[:, sl], p.psA[0][:, 0:W])
                            v.tensor_mul(p.vtmp[:, 0:W], p.vtmp[:, 0:W], p.psA[1][:, 0:W])
                            v.tensor_scalar(xn[:, sl], p.vtmp[:, 0:W],
                                            gcol[:, dc:dc + 1], bcol[:, dc:dc + 1],
                                            ALU.mult, ALU.add).then_inc(p.vS.h, 1)
                            p.vS.n += 1

            for d in "fb":
                stage_a_mamba(p, d, xn, scr_xc[d], scr_sz[d], dbltmp[d])
                evac_dbl(p, d, dbltmp[d], dtvec[d])

        for d in "fb":
            with contextlib.ExitStack() as esd:
                dtwT = esd.enter_context(
                    nc.sbuf_tensor("dtwT_" + d, [64, 16 * 128], BF16))
                prep_dt_tiles(p, d, dtwT)
                stage_c_mamba(p, d, dbltmp[d], dtvec[d], dtwT, state,
                              scr_xc[d], scr_sz[d], scr_y2[d])
            out_proj_mamba(p, d, scr_y2[d], scr_yf)


def xn_mv(p, xn, kc, j0, w, rev):
    if not rev:
        return xn[:, kc * T + j0: kc * T + j0 + w]
    hi = kc * T + (T - 1 - j0)
    lo = kc * T + (T - j0 - w)
    return xn[:, hi: lo - 1 if lo > 0 else None: -1] if lo > 0 else xn[:, hi::-1]


def stage_a_mamba(p, d, xn, sxc, ssz, dbl):
    nc = p.nc
    rev = (d == "b")
    inw = p.wscr["L0_attn_" + d + "_in_w"]
    xpw = p.wscr["L0_attn_" + d + "_xproj_w"]
    cvw = p.convw[d]
    cvb = p.cols["L0_attn_" + d + "_conv_b"]
    W = 512
    with contextlib.ExitStack() as es:
        slab = es.enter_context(nc.sbuf_tensor("ma_slab_" + d, [128, 8 * 1024], BF16))
        xpt = es.enter_context(nc.sbuf_tensor("ma_xpt_" + d, [128, 2 * 96], BF16))
        xip = es.enter_context(nc.sbuf_tensor("ma_xip_" + d, [128, 2 * (3 + T)], F32))
        cvt = es.enter_context(nc.sbuf_tensor("ma_cvt_" + d, [128, 2 * T], F32))
        xcb = es.enter_context(nc.sbuf_tensor("ma_xcb_" + d, [128, T], BF16))

        for q in range(4):
            is_xi = q < 2
            with nc.Block() as blk:
                qw0 = [p.qW[i].n for i in range(4)]
                qy0 = [p.qY[i].n for i in range(2)]
                qx0 = [p.qX[i].n for i in range(2)]
                t_base = p.tS.n
                s_base = p.sS.n
                v_base = p.vS.n
                TPS = 36 if is_xi else 32
                SPS = 6 if is_xi else 4

                @blk.sync
                def _(sync):
                    for kc in range(8):
                        sync.dma_start(slab[:, kc * 1024:(kc + 1) * 1024],
                                       inw[q * 1024:(q + 1) * 1024,
                                           kc * 128:(kc + 1) * 128],
                                       transpose=True).then_inc(p.qW[kc % 4].h, 16)
                        p.qW[kc % 4].n += 16
                    for s in range(8):
                        gs = (q % 2) * 8 + s
                        if is_xi:
                            if s >= 2:
                                sync.wait_ge(p.tS.h, t_base + TPS * (s - 2) + TPS)
                            sync.dma_start(xpt[:, (s % 2) * 96:(s % 2) * 96 + 96],
                                           xpw[0:96, (q * 8 + s) * 128:
                                               (q * 8 + s + 1) * 128],
                                           transpose=True).then_inc(p.qX[s % 2].h, 16)
                            p.qX[s % 2].n += 16
                        sync.wait_ge(p.sS.h, s_base + SPS * s + (5 if is_xi else 4))
                        dst = sxc if is_xi else ssz
                        sync.dma_start(dst[gs * 128:(gs + 1) * 128, :],
                                       cvt[:, (0 if is_xi else (s % 2)) * T:
                                           (0 if is_xi else (s % 2)) * T + T])\
                            .then_inc(p.qY[0].h, 16)
                        p.qY[0].n += 16
                    sync.wait_ge(p.qY[0].h, p.qY[0].n)

                @blk.tensor
                def _(t):
                    for kc in range(8):
                        t.wait_ge(p.qW[kc % 4].h, qw0[kc % 4] + 16 * (kc // 4 + 1))
                    for s in range(8):
                        gs = q * 8 + s
                        for tcc in range(4):
                            u = s * 4 + tcc
                            if u >= 2:
                                u2 = u - 2
                                t.wait_ge(p.sS.h, s_base + (u2 // 4) * SPS + (u2 % 4) + 1)
                            for kc in range(8):
                                t.matmul(p.psA[u % 2][:, 0:W],
                                         slab[:, kc * 1024 + s * 128: kc * 1024 + s * 128 + 128],
                                         xn_mv(p, xn, kc, tcc * W, W, rev),
                                         start=(kc == 0), stop=(kc == 7)).then_inc(p.tS.h, 1)
                                p.tS.n += 1
                        if is_xi:
                            t.wait_ge(p.qX[s % 2].h, qx0[s % 2] + 16 * (s // 2 + 1))
                            t.wait_ge(p.sS.h, s_base + SPS * s + 6)
                            for tcc in range(4):
                                t.matmul(p.psB[0:96, tcc * W:(tcc + 1) * W],
                                         xpt[:, (s % 2) * 96:(s % 2) * 96 + 96],
                                         xcb[:, tcc * W:(tcc + 1) * W],
                                         start=(gs == 0), stop=(gs == 15))\
                                    .then_inc(p.tS.h, 1)
                                p.tS.n += 1

                @blk.scalar
                def _(s_):
                    for s in range(8):
                        if is_xi:
                            if s >= 2:
                                s_.wait_ge(p.vS.h, v_base + (s - 2) + 1)
                            for tcc in range(4):
                                u = s * 4 + tcc
                                s_.wait_ge(p.tS.h, t_base + TPS * s + 8 * (tcc + 1))
                                s_.copy(xip[:, (s % 2) * (3 + T) + 3 + tcc * W:
                                            (s % 2) * (3 + T) + 3 + tcc * W + W],
                                        p.psA[u % 2][:, 0:W]).then_inc(p.sS.h, 1)
                                p.sS.n += 1
                            s_.wait_ge(p.vS.h, v_base + s + 1)
                            if s >= 1:
                                s_.wait_ge(p.tS.h, t_base + TPS * (s - 1) + TPS)
                            s_.activation(cvt[:, 0:T], cvt[:, 0:T], AF.Silu)\
                                .then_inc(p.sS.h, 1)
                            s_.activation(xcb[:, 0:T], cvt[:, 0:T], AF.Copy)\
                                .then_inc(p.sS.h, 1)
                            p.sS.n += 2
                        else:
                            for tcc in range(4):
                                u = s * 4 + tcc
                                s_.wait_ge(p.tS.h, t_base + TPS * s + 8 * (tcc + 1))
                                if tcc == 0 and s >= 2:
                                    s_.wait_ge(p.qY[0].h, qy0[0] + 16 * (s - 1))
                                s_.activation(cvt[:, (s % 2) * T + tcc * W:
                                                  (s % 2) * T + tcc * W + W],
                                              p.psA[u % 2][:, 0:W], AF.Silu)\
                                    .then_inc(p.sS.h, 1)
                                p.sS.n += 1

                if is_xi:
                    @blk.vector
                    def _(v):
                        for s in range(8):
                            gs = q * 8 + s
                            base = (s % 2) * (3 + T)
                            v.memset(xip[:, base:base + 3], 0.0)
                            v.wait_ge(p.sS.h, s_base + SPS * s + 4)
                            if s >= 1:
                                v.wait_ge(p.qY[0].h, qy0[0] + 16 * s)
                                v.wait_ge(p.sS.h, s_base + SPS * (s - 1) + 6)
                            v.tensor_scalar(cvt[:, 0:T], xip[:, base:base + T],
                                            cvw[:, gs * 4:gs * 4 + 1],
                                            cvb[:, gs:gs + 1], ALU.mult, ALU.add)
                            for k in range(1, 4):
                                v.scalar_tensor_tensor(cvt[:, 0:T],
                                                       xip[:, base + k:base + k + T],
                                                       cvw[:, gs * 4 + k:gs * 4 + k + 1],
                                                       cvt[:, 0:T], ALU.mult, ALU.add)
                            v.engine_nop().then_inc(p.vS.h, 1)
                            p.vS.n += 1


def evac_dbl(p, d, dbl, dtv):
    nc = p.nc
    with nc.Block() as blk:
        @blk.scalar
        def _(s):
            for tcc in range(4):
                s.copy(dbl[0:96, tcc * 512:(tcc + 1) * 512],
                       p.psB[0:96, tcc * 512:(tcc + 1) * 512]).then_inc(p.sS.h, 1)
                p.sS.n += 1

        @blk.vector
        def _(v):
            v.wait_ge(p.sS.h, p.sS.n)
            for tcc in range(4):
                v.tensor_copy(dtv[0:64, tcc * 512:(tcc + 1) * 512],
                              dbl[0:64, tcc * 512:(tcc + 1) * 512]).then_inc(p.vS.h, 1)
                p.vS.n += 1


def prep_dt_tiles(p, d, dtw):
    nc = p.nc
    dts = p.wscr["L0_attn_" + d + "_dt_w"]
    with nc.sbuf_tensor("dtstage_" + d, [128, 64], BF16) as dstage:
        with nc.Block() as blk:
            q_base = p.qS.n
            t_base = p.tS.n
            s_base = p.sS.n

            @blk.sync
            def _(sync):
                for s in range(16):
                    if s >= 1:
                        sync.wait_ge(p.tS.h, t_base + s)
                    sync.dma_start(dstage[:, 0:64],
                                   dts[s * 128:(s + 1) * 128, 0:64]).then_inc(p.qS.h, 16)
                    p.qS.n += 16

            @blk.tensor
            def _(t):
                for s in range(16):
                    t.wait_ge(p.qS.h, q_base + 16 * (s + 1))
                    if s >= 2:
                        t.wait_ge(p.sS.h, s_base + s - 1)
                    dstp = (p.psC if s % 2 == 0 else p.psD)
                    t.transpose(dstp[0:64, 0:64].bitcast(BF16), dstage[:, 0:64],
                                p.identb_t[:]).then_inc(p.tS.h, 1)
                    p.tS.n += 1

            @blk.scalar
            def _(s_):
                for s in range(16):
                    s_.wait_ge(p.tS.h, t_base + s + 1)
                    srcp = (p.psC if s % 2 == 0 else p.psD)
                    s_.copy(dtw[0:64, s * 128:(s + 1) * 128],
                            srcp[0:64, 0:64].bitcast(BF16)).then_inc(p.sS.h, 1)
                    p.sS.n += 1


def stage_c_mamba(p, d, dbl, dtv, dtw, state, sxc, ssz, sy2):
    nc = p.nc
    W = 512
    dtb = p.cols["L0_attn_" + d + "_dt_b"]
    Dsk = p.cols["L0_attn_" + d + "_D_skip"]
    Acl = p.Acol[d]
    SPS = 20
    VPS = 20
    TPS = 17
    GPS = 8
    with contextlib.ExitStack() as es:
        bb = es.enter_context(nc.sbuf_tensor("mc_bb_" + d, [128, 32 * W], F32))
        xcs = es.enter_context(nc.sbuf_tensor("mc_xcs_" + d, [128, 2 * W], F32))
        szs = es.enter_context(nc.sbuf_tensor("mc_szs_" + d, [128, 2 * W], F32))
        dlt = es.enter_context(nc.sbuf_tensor("mc_dlt_" + d, [128, W], F32))
        dBt = es.enter_context(nc.sbuf_tensor("mc_dBt_" + d, [128, W], F32))
        ht = es.enter_context(nc.sbuf_tensor("mc_ht_" + d, [128, 6 * W], F32))
        ut = p.psB
        dAt = p.psD
        sp1 = p.psB[:, W:2 * W]
        pbf = es.enter_context(nc.sbuf_tensor("mc_pbf_" + d, [128, 4 * W], BF16))
        y2t = es.enter_context(nc.sbuf_tensor("mc_y2t_" + d, [128, 2 * W], BF16))

        with nc.Block() as blk:
            @blk.vector
            def _(v):
                v.memset(state[:], 0.0)

        for tcc in range(4):
            with nc.Block() as blk:
                t_base = p.tS.n
                s_base = p.sS.n

                @blk.tensor
                def _(t):
                    for n in range(32):
                        if n >= 2:
                            t.wait_ge(p.sS.h, s_base + n - 1)
                        sel = p.identf_t[0:96, 64 + n:65 + n].broadcast_to([96, 128])
                        t.matmul(p.psA[n % 2][:, 0:W], sel,
                                 dbl[0:96, tcc * W:(tcc + 1) * W],
                                 start=True, stop=True).then_inc(p.tS.h, 1)
                        p.tS.n += 1

                @blk.scalar
                def _(s_):
                    for n in range(32):
                        s_.wait_ge(p.tS.h, t_base + n + 1)
                        s_.copy(bb[:, n * W:(n + 1) * W], p.psA[n % 2][:, 0:W])\
                            .then_inc(p.sS.h, 1)
                        p.sS.n += 1

            with nc.Block() as blk:
                qx_base = [p.qX[i].n for i in range(2)]
                qz_base = [p.qZ[i].n for i in range(2)]
                qy_base = [p.qY[i].n for i in range(2)]
                t_base = p.tS.n
                s_base = p.sS.n
                v_base = p.vS.n
                g_base = p.gS.n

                @blk.sync
                def _(sync):
                    for s in range(16):
                        sl = s % 2
                        if s >= 2:
                            sync.wait_ge(p.vS.h, v_base + VPS * (s - 2) + VPS)
                        sync.dma_start(xcs[:, sl * W:(sl + 1) * W],
                                       sxc[s * 128:(s + 1) * 128, tcc * W:(tcc + 1) * W])\
                            .then_inc(p.qX[sl].h, 16)
                        p.qX[sl].n += 16
                        sync.dma_start(szs[:, sl * W:(sl + 1) * W],
                                       ssz[s * 128:(s + 1) * 128, tcc * W:(tcc + 1) * W])\
                            .then_inc(p.qZ[sl].h, 16)
                        p.qZ[sl].n += 16
                        sync.wait_ge(p.vS.h, v_base + VPS * s + VPS)
                        sync.dma_start(sy2[s * 128:(s + 1) * 128, tcc * W:(tcc + 1) * W],
                                       y2t[:, sl * W:(sl + 1) * W]).then_inc(p.qY[sl].h, 16)
                        p.qY[sl].n += 16
                    sync.wait_ge(p.qY[0].h, p.qY[0].n)
                    sync.wait_ge(p.qY[1].h, p.qY[1].n)

                @blk.tensor
                def _(t):
                    for s in range(16):
                        if s >= 2:
                            t.wait_ge(p.sS.h, s_base + SPS * (s - 2) + 4)
                        t.matmul(p.psA[s % 2][:, 0:W], dtw[0:64, s * 128:(s + 1) * 128],
                                 dtv[0:64, tcc * W:(tcc + 1) * W],
                                 start=True, stop=True).then_inc(p.tS.h, 1)
                        p.tS.n += 1
                        for n in range(16):
                            t.wait_ge(p.gS.h, g_base + GPS * s + n // 2 + 1)
                            if n == 0 and s >= 1:
                                t.wait_ge(p.vS.h, v_base + VPS * (s - 1) + 19)
                            kp = (GPS * s + n // 2) % 2
                            t.matmul(p.psC[:, 0:W], p.identb_t[:],
                                     pbf[:, kp * 2 * W + (n % 2) * W:
                                         kp * 2 * W + (n % 2) * W + W],
                                     start=(n == 0), stop=(n == 15)).then_inc(p.tS.h, 1)
                            p.tS.n += 1

                @blk.scalar
                def _(s_):
                    for s in range(16):
                        sb = s_base + SPS * s
                        if s >= 1:
                            s_.wait_ge(p.vS.h, v_base + VPS * (s - 1) + 2)
                        s_.wait_ge(p.tS.h, t_base + TPS * s + 1)
                        s_.activation(sp1, p.psA[s % 2][:, 0:W], AF.Abs,
                                      bias=dtb[:, s:s + 1]).then_inc(p.sS.h, 1)
                        s_.activation(sp1, sp1, AF.Exp, scale=-1.0)\
                            .then_inc(p.sS.h, 1)
                        s_.activation(sp1, sp1, AF.Ln, bias=1.0)\
                            .then_inc(p.sS.h, 1)
                        s_.activation(dlt[:, 0:W], p.psA[s % 2][:, 0:W], AF.Relu,
                                      bias=dtb[:, s:s + 1]).then_inc(p.sS.h, 1)
                        p.sS.n += 4
                        s_.wait_ge(p.vS.h, v_base + VPS * s + 1)
                        for n in range(16):
                            if n >= 1:
                                s_.wait_ge(p.vS.h, v_base + VPS * s + 2 + n)
                            s_.activation(dAt[:, 0:W], dlt[:, 0:W],
                                          AF.Exp, scale=Acl[:, s * 16 + n:s * 16 + n + 1])\
                                .then_inc(p.sS.h, 1)
                            p.sS.n += 1

                @blk.vector
                def _(v):
                    for s in range(16):
                        sl = s % 2
                        vb = v_base + VPS * s
                        sb = s_base + SPS * s
                        v.wait_ge(p.sS.h, sb + 4)
                        v.tensor_add(dlt[:, 0:W], dlt[:, 0:W], sp1)\
                            .then_inc(p.vS.h, 1)
                        v.wait_ge(p.qX[sl].h, qx_base[sl] + 16 * (s // 2 + 1))
                        v.tensor_mul(ut[:, 0:W], dlt[:, 0:W], xcs[:, sl * W:(sl + 1) * W])\
                            .then_inc(p.vS.h, 1)
                        p.vS.n += 2
                        for n in range(16):
                            v.tensor_mul(dBt[:, 0:W], ut[:, 0:W], bb[:, n * W:(n + 1) * W])
                            v.wait_ge(p.sS.h, sb + 4 + n + 1)
                            idx = s * 16 + n
                            v.scalar_tensor_tensor(dBt[:, 0:1], dAt[:, 0:1],
                                                   state[:, idx:idx + 1], dBt[:, 0:1],
                                                   ALU.mult, ALU.add)
                            sl6 = n % 6
                            kk = 16 * s + n - 6
                            if kk >= 0:
                                v.wait_ge(p.gS.h, g_base + (kk // 16) * GPS
                                          + (kk % 16) // 2 + 1)
                            v.tensor_tensor_scan(ht[:, sl6 * W:(sl6 + 1) * W],
                                                 dAt[:, 0:W],
                                                 dBt[:, 0:W], 0.0,
                                                 ALU.mult, ALU.add).then_inc(p.vS.h, 1)
                            v.tensor_copy(state[:, idx:idx + 1],
                                          ht[:, sl6 * W + W - 1:sl6 * W + W])
                            p.vS.n += 1
                        v.wait_ge(p.tS.h, t_base + TPS * s + TPS)
                        v.scalar_tensor_tensor(dBt[:, 0:W], xcs[:, sl * W:(sl + 1) * W],
                                               Dsk[:, s:s + 1], p.psC[:, 0:W],
                                               ALU.mult, ALU.add).then_inc(p.vS.h, 1)
                        v.wait_ge(p.qZ[sl].h, qz_base[sl] + 16 * (s // 2 + 1))
                        if s >= 2:
                            v.wait_ge(p.qY[sl].h, qy_base[sl] + 16 * (s // 2))
                        v.tensor_mul(y2t[:, sl * W:(sl + 1) * W], dBt[:, 0:W],
                                     szs[:, sl * W:(sl + 1) * W]).then_inc(p.vS.h, 1)
                        p.vS.n += 2

                @blk.gpsimd
                def _(g):
                    for s in range(16):
                        for j in range(8):
                            k = GPS * s + j
                            g.wait_ge(p.vS.h, v_base + VPS * s + 2 + 2 * j + 2)
                            if k >= 2:
                                kk = k - 2
                                g.wait_ge(p.tS.h, t_base + TPS * (kk // 8)
                                          + 1 + (2 * (kk % 8) + 1) + 1)
                            jj = j % 3
                            g.tensor_mul(pbf[:, (k % 2) * 2 * W:(k % 2) * 2 * W + 2 * W],
                                         ht[:, 2 * jj * W:(2 * jj + 2) * W],
                                         bb[:, (16 + 2 * j) * W:(18 + 2 * j) * W])\
                                .then_inc(p.gS.h, 1)
                            p.gS.n += 1


def out_proj_mamba(p, d, sy2, syf):
    nc = p.nc
    W = 512
    ows = p.wscr["L0_attn_" + d + "_out_w"]
    rev = (d == "b")
    with contextlib.ExitStack() as es:
        slabs = es.enter_context(nc.sbuf_tensor("op_slabs_" + d, [128, 16 * 1024], BF16))
        y2b = es.enter_context(nc.sbuf_tensor("op_y2b_" + d, [128, 16 * W], BF16))
        opl = es.enter_context(nc.sbuf_tensor("op_opl_" + d, [128, 8 * W], F32))
        yfb = es.enter_context(nc.sbuf_tensor("op_yfb_" + d, [128, 8 * W], F32))

        for tcc in range(4):
            ntc = 3 - tcc if rev else tcc
            with nc.Block() as blk:
                @blk.sync
                def _(sync):
                    for kc in range(16):
                        sync.dma_start(slabs[:, kc * 1024:(kc + 1) * 1024],
                                       ows[0:1024, kc * 128:(kc + 1) * 128],
                                       transpose=True).then_inc(p.qS.h, 16)
                        p.qS.n += 16
                        sync.dma_start(y2b[:, kc * W:(kc + 1) * W],
                                       sy2[kc * 128:(kc + 1) * 128, tcc * W:(tcc + 1) * W])\
                            .then_inc(p.qS.h, 16)
                        p.qS.n += 16
                    if rev:
                        for m in range(8):
                            sync.dma_start(yfb[:, m * W:(m + 1) * W],
                                           syf[m * 128:(m + 1) * 128,
                                               ntc * W:(ntc + 1) * W]).then_inc(p.qS.h, 16)
                            p.qS.n += 16
                    sync.wait_ge(p.qS.h, p.qS.n)

            with nc.Block() as blk:
                t_base = p.tS.n
                s_base = p.sS.n
                v_base = p.vS.n
                q_base = p.qS.n

                @blk.tensor
                def _(t):
                    for m in range(8):
                        if m >= 2:
                            if rev:
                                t.wait_ge(p.vS.h, v_base + m - 1)
                            else:
                                t.wait_ge(p.sS.h, s_base + m - 1)
                        for kc in range(16):
                            t.matmul(p.psA[m % 2][:, 0:W],
                                     slabs[:, kc * 1024 + m * 128: kc * 1024 + m * 128 + 128],
                                     y2b[:, kc * W:(kc + 1) * W],
                                     start=(kc == 0), stop=(kc == 15)).then_inc(p.tS.h, 1)
                            p.tS.n += 1

                if not rev:
                    @blk.scalar
                    def _(s_):
                        for m in range(8):
                            s_.wait_ge(p.tS.h, t_base + 16 * (m + 1))
                            s_.activation(opl[:, m * W:(m + 1) * W], p.psA[m % 2][:, 0:W],
                                          AF.Copy, scale=0.5).then_inc(p.sS.h, 1)
                            p.sS.n += 1

                    @blk.sync
                    def _(sync):
                        for m in range(8):
                            sync.wait_ge(p.sS.h, s_base + m + 1)
                            sync.dma_start(syf[m * 128:(m + 1) * 128, tcc * W:(tcc + 1) * W],
                                           opl[:, m * W:(m + 1) * W]).then_inc(p.qS.h, 16)
                            p.qS.n += 16
                        sync.wait_ge(p.qS.h, p.qS.n)
                else:
                    @blk.vector
                    def _(v):
                        for m in range(8):
                            v.wait_ge(p.tS.h, t_base + 16 * (m + 1))
                            v.scalar_tensor_tensor(opl[:, m * W:(m + 1) * W],
                                                   p.psA[m % 2][:, W - 1::-1], 0.5,
                                                   yfb[:, m * W:(m + 1) * W],
                                                   ALU.mult, ALU.add).then_inc(p.vS.h, 1)
                            p.vS.n += 1

            if rev:
                post_ln_residual_tc(p, ntc, lambda dc: opl[:, dc * W:(dc + 1) * W])


def build_l1_attn(p):
    nc = p.nc
    TC = 256
    LNK = FC(math.log(1.0 / 16.0))
    with contextlib.ExitStack() as es:
        xn3 = es.enter_context(nc.sbuf_tensor("ga_xn", [128, 8 * T], BF16))
        qpl = es.enter_context(nc.sbuf_tensor("ga_q", [128, 8 * TC], BF16))
        kpl = es.enter_context(nc.sbuf_tensor("ga_k", [128, 8 * TC], BF16))
        vpl = es.enter_context(nc.sbuf_tensor("ga_v", [128, 2 * 2048], BF16))
        gsc = es.enter_context(nc.sbuf_tensor("ga_gsc", [128, 16 * TC], F32))
        opl = es.enter_context(nc.sbuf_tensor("ga_o", [128, 16 * TC], F32))
        Spl = es.enter_context(nc.sbuf_tensor("ga_S", [128, 4096], F32))
        Sb = es.enter_context(nc.sbuf_tensor("ga_Sb", [128, 4096], BF16))
        wsl = es.enter_context(nc.sbuf_tensor("ga_wsl", [128, 4096], BF16))
        aol = gsc
        gk1 = es.enter_context(nc.sbuf_tensor("ga_gk1", [16, TC], BF16))
        e1 = es.enter_context(nc.sbuf_tensor("ga_e1", [128, 128], F32))
        e2 = es.enter_context(nc.sbuf_tensor("ga_e2", [128, 128], F32))
        kdf = es.enter_context(nc.sbuf_tensor("ga_kdf", [128, 128], F32))
        qgb = es.enter_context(nc.sbuf_tensor("ga_qgb", [128, 128], BF16))
        kdb = es.enter_context(nc.sbuf_tensor("ga_kdb", [128, 128], BF16))
        kgb = es.enter_context(nc.sbuf_tensor("ga_kgb", [128, 128], BF16))
        atb = es.enter_context(nc.sbuf_tensor("ga_atb", [128, 64], BF16))
        kgtb = es.enter_context(nc.sbuf_tensor("ga_kgtb", [128, 256], BF16))
        e4c = es.enter_context(nc.sbuf_tensor("ga_e4c", [128, 8], F32))
        onest = es.enter_context(nc.sbuf_tensor("ga_ones", [128, 64], F32))
        bgn = es.enter_context(nc.sbuf_tensor("ga_bgn", [128, 8], F32))

        plain_preln_plane(p, xn3)

        with nc.Block() as blk:
            @blk.vector
            def _(v):
                v.memset(Spl[:], 0.0)
                v.memset(Sb[:], 0.0)
                v.memset(onest[:], 1.0)
                v.tensor_scalar(bgn[:], p.cols["L1_attn_bgk2"][:], -1.0, None, ALU.mult)

        if DEBUG_GLA:
            dbg = {}
            for nm, shp, dt in [("g_q", [128, 8 * TC], BF16), ("g_k", [128, 8 * TC], BF16),
                                ("g_v", [128, 2 * 2048], BF16), ("g_gsc", [128, 16 * TC], F32),
                                ("g_opl", [128, 16 * TC], F32), ("g_aol", [128, 8 * TC], F32)]:
                dbg[nm] = nc.dram_tensor(nm, shp, dt, kind="ExternalOutput").ap()
        for tc in range(T // TC):
            gla_proj(p, tc, xn3, qpl, kpl, vpl, gsc, gk1, wsl, bgn, onest)
            if DEBUG_GLA and tc == 0:
                with nc.Block() as blk:
                    @blk.sync
                    def _(sync):
                        for nm, src_t in [("g_q", qpl), ("g_k", kpl), ("g_v", vpl),
                                          ("g_gsc", gsc)]:
                            sync.dma_start(dbg[nm], src_t[:]).then_inc(p.qS.h, 16)
                            p.qS.n += 16
                        sync.wait_ge(p.qS.h, p.qS.n)
            gla_inner(p, tc, qpl, kpl, vpl, gsc, opl, Spl, Sb,
                      e1, e2, kdf, qgb, kdb, kgb, atb, kgtb, e4c, LNK)
            if DEBUG_GLA and tc == 0:
                with nc.Block() as blk:
                    @blk.sync
                    def _(sync):
                        sync.dma_start(dbg["g_opl"], opl[:]).then_inc(p.qS.h, 16)
                        p.qS.n += 16
                        sync.wait_ge(p.qS.h, p.qS.n)
            gla_final(p, tc, xn3, opl, gsc, aol, wsl, qpl, kpl)
            if DEBUG_GLA and tc == 0:
                with nc.Block() as blk:
                    @blk.sync
                    def _(sync):
                        sync.dma_start(dbg["g_aol"], aol[:, 0:8 * TC]).then_inc(p.qS.h, 16)
                        p.qS.n += 16
                        sync.wait_ge(p.qS.h, p.qS.n)
            post_ln_residual_tc(p, tc, lambda dc: aol[:, dc * TC:(dc + 1) * TC],
                                tcw=TC)


def gla_proj(p, tc, xn3, qpl, kpl, vpl, gsc, gk1, wsl, bgn, onest):
    nc = p.nc
    TC, CH = 256, 64

    def xmov(kc, w=TC, off=0):
        return xn3[:, kc * T + tc * TC + off: kc * T + tc * TC + off + w]

    for wname, outpl in [("L1_attn_Wq", qpl), ("L1_attn_Wk", kpl)]:
        wscr = p.wscr[wname]
        with nc.Block() as blk:
            t_base = p.tS.n
            s_base = p.sS.n
            qw_base = [p.qW[i].n for i in range(4)]

            @blk.sync
            def _(sync):
                for m in range(8):
                    for kc in range(8):
                        j = m * 8 + kc
                        slot = j % 4
                        if j >= 4:
                            sync.wait_ge(p.tS.h, t_base + (j - 3))
                        sync.dma_start(wsl[:, slot * 128:(slot + 1) * 128],
                                       wscr[kc * 128:(kc + 1) * 128,
                                            m * 128:(m + 1) * 128])\
                            .then_inc(p.qW[slot].h, 16)
                        p.qW[slot].n += 16

            @blk.tensor
            def _(t):
                for m in range(8):
                    if m >= 2:
                        t.wait_ge(p.sS.h, s_base + (m - 1))
                    for kc in range(8):
                        j = m * 8 + kc
                        slot = j % 4
                        t.wait_ge(p.qW[slot].h, qw_base[slot] + 16 * (j // 4 + 1))
                        t.matmul(p.psA[m % 2][:, 0:TC],
                                 wsl[:, slot * 128:(slot + 1) * 128], xmov(kc),
                                 start=(kc == 0), stop=(kc == 7)).then_inc(p.tS.h, 1)
                        p.tS.n += 1

            @blk.scalar
            def _(s_):
                for m in range(8):
                    s_.wait_ge(p.tS.h, t_base + 8 * (m + 1))
                    s_.copy(outpl[:, m * TC:(m + 1) * TC],
                            p.psA[m % 2][:, 0:TC]).then_inc(p.sS.h, 1)
                    p.sS.n += 1

    with nc.Block() as blk:
        q_base = p.qS.n
        t_base = p.tS.n

        @blk.sync
        def _(sync):
            for kc in range(8):
                sync.dma_start(wsl[:, kc * 16:(kc + 1) * 16],
                               p.wscr["L1_attn_Wgk1"][kc * 128:(kc + 1) * 128, 0:16])\
                    .then_inc(p.qS.h, 16)
                p.qS.n += 16
            sync.wait_ge(p.qS.h, p.qS.n)

        @blk.tensor
        def _(t):
            t.wait_ge(p.qS.h, q_base + 16 * 8)
            for kc in range(8):
                t.matmul(p.psD[0:16, 0:TC], wsl[:, kc * 16:(kc + 1) * 16], xmov(kc),
                         start=(kc == 0), stop=(kc == 7)).then_inc(p.tS.h, 1)
                p.tS.n += 1

        @blk.scalar
        def _(s_):
            s_.wait_ge(p.tS.h, t_base + 8)
            s_.copy(gk1[0:16, 0:TC], p.psD[0:16, 0:TC]).then_inc(p.sS.h, 1)
            p.sS.n += 1

    with nc.Block() as blk:
        q_base = p.qS.n
        t_base = p.tS.n
        s_base = p.sS.n
        v_base = p.vS.n

        @blk.sync
        def _(sync):
            sync.dma_start(wsl[0:16, 0:1024], p.wscr["L1_attn_Wgk2"][0:16, 0:1024])\
                .then_inc(p.qS.h, 16)
            p.qS.n += 16
            sync.wait_ge(p.qS.h, p.qS.n)

        @blk.tensor
        def _(t):
            t.wait_ge(p.qS.h, q_base + 16)
            for m in range(8):
                if m >= 2:
                    t.wait_ge(p.sS.h, s_base + 4 * (m - 1))
                t.matmul(p.psA[m % 2][:, 0:TC], wsl[0:16, m * 128:(m + 1) * 128],
                         gk1[0:16, 0:TC], start=True, stop=True).then_inc(p.tS.h, 1)
                p.tS.n += 1

        @blk.scalar
        def _(s_):
            for m in range(8):
                s_.wait_ge(p.tS.h, t_base + m + 1)
                if m >= 1:
                    s_.wait_ge(p.vS.h, v_base + m)
                s_.activation(p.sqtile[:, 0:TC], p.psA[m % 2][:, 0:TC], AF.Abs,
                              bias=p.cols["L1_attn_bgk2"][:, m:m + 1]).then_inc(p.sS.h, 1)
                s_.activation(p.sqtile[:, 0:TC], p.sqtile[:, 0:TC], AF.Exp, scale=-1.0)\
                    .then_inc(p.sS.h, 1)
                s_.activation(p.sqtile[:, 0:TC], p.sqtile[:, 0:TC], AF.Ln, bias=1.0)\
                    .then_inc(p.sS.h, 1)
                s_.activation(gsc[:, (8 + m) * TC:(9 + m) * TC],
                              p.psA[m % 2][:, 0:TC], AF.Relu,
                              scale=-1.0, bias=bgn[:, m:m + 1]).then_inc(p.sS.h, 1)
                p.sS.n += 4

        @blk.vector
        def _(v):
            for m in range(8):
                v.wait_ge(p.sS.h, s_base + 4 * (m + 1))
                v.tensor_add(gsc[:, (8 + m) * TC:(9 + m) * TC],
                             gsc[:, (8 + m) * TC:(9 + m) * TC],
                             p.sqtile[:, 0:TC]).then_inc(p.vS.h, 1)
                p.vS.n += 1
            for m in range(8):
                for c in range(4):
                    sl = slice((8 + m) * TC + c * CH, (8 + m) * TC + (c + 1) * CH)
                    v.tensor_tensor_scan(gsc[:, sl], onest[:, 0:CH], gsc[:, sl],
                                         0.0, ALU.mult, ALU.add).then_inc(p.vS.h, 1)
                    p.vS.n += 1

    with nc.Block() as blk:
        t_base = p.tS.n
        qv_base = [p.qW[i].n for i in range(2)]

        def psv(tt, dvs):
            i = tt * 4 + dvs
            if i < 2:
                return p.psA[i][:, 0:512]
            if i < 6:
                return p.psB[:, (i - 2) * 512:(i - 1) * 512]
            return (p.psC if i == 6 else p.psD)[:, 0:512]

        @blk.sync
        def _(sync):
            for kc in range(8):
                slot = kc % 2
                if kc >= 2:
                    sync.wait_ge(p.tS.h, t_base + 8 * (kc - 1))
                sync.dma_start(wsl[:, slot * 2048:(slot + 1) * 2048],
                               p.wscr["L1_attn_Wv"][kc * 128:(kc + 1) * 128, 0:2048])\
                    .then_inc(p.qW[slot].h, 16)
                p.qW[slot].n += 16

        @blk.tensor
        def _(t):
            for kc in range(8):
                slot = kc % 2
                t.wait_ge(p.qW[slot].h, qv_base[slot] + 16 * (kc // 2 + 1))
                for tt in range(2):
                    for dvs in range(4):
                        t.matmul(psv(tt, dvs), xmov(kc, 128, tt * 128),
                                 wsl[:, slot * 2048 + dvs * 512:
                                     slot * 2048 + (dvs + 1) * 512],
                                 start=(kc == 0), stop=(kc == 7)).then_inc(p.tS.h, 1)
                        p.tS.n += 1

        @blk.scalar
        def _(s_):
            s_.wait_ge(p.tS.h, t_base + 64)
            for tt in range(2):
                for dvs in range(4):
                    s_.copy(vpl[:, tt * 2048 + dvs * 512: tt * 2048 + (dvs + 1) * 512],
                            psv(tt, dvs)).then_inc(p.sS.h, 1)
                    p.sS.n += 1


def gla_inner(p, tc, qpl, kpl, vpl, gsc, opl, Spl, Sb,
              e1, e2, kdf, qgb, kdb, kgb, atb, kgtb, e4c, LNK):
    nc = p.nc
    TC, CH = 256, 64
    SP = 11
    VP = 8
    TP = 18
    with nc.Block() as blk:
        t_base = p.tS.n
        s_base = p.sS.n
        v_base = p.vS.n

        def sbase(c, h):
            return s_base + (c + 1) + SP * (4 * c + h)

        @blk.scalar
        def _(s_):
            for c in range(4):
                ap = gsc[:, 8 * TC + c * CH + CH - 1:: TC]
                if c >= 1:
                    s_.wait_ge(p.vS.h, v_base + VP * (4 * c))
                s_.activation(e4c[:, 0:8], ap, AF.Exp, scale=FC(-1.0 / 16.0))\
                    .then_inc(p.sS.h, 1)
                p.sS.n += 1
                for h in range(4):
                    khc = 4 * c + h
                    sb = sbase(c, h)
                    if khc >= 1:
                        s_.wait_ge(p.vS.h, v_base + VP * khc)
                    for ph in range(2):
                        sl = slice((8 + 2 * h + ph) * TC + c * CH,
                                   (8 + 2 * h + ph) * TC + (c + 1) * CH)
                        s_.activation(e1[:, ph * CH:(ph + 1) * CH], gsc[:, sl], AF.Exp,
                                      scale=FC(-1.0 / 16.0), bias=LNK).then_inc(p.sS.h, 1)
                        s_.activation(e2[:, ph * CH:(ph + 1) * CH], gsc[:, sl], AF.Exp,
                                      scale=FC(1.0 / 16.0)).then_inc(p.sS.h, 1)
                        p.sS.n += 2
                    s_.wait_ge(p.tS.h, t_base + TP * khc + 14)
                    for dvc in range(4):
                        s_.copy(opl[:, (4 * h + dvc) * TC + c * CH:
                                    (4 * h + dvc) * TC + (c + 1) * CH],
                                p.psA[0][:, dvc * CH:(dvc + 1) * CH]).then_inc(p.sS.h, 1)
                        p.sS.n += 1
                    s_.wait_ge(p.tS.h, t_base + TP * khc + 16)
                    row_s = (c % 2) * 64
                    for i in range(2):
                        s_.copy(kgtb[row_s:row_s + 64, i * 128:(i + 1) * 128],
                                p.psA[1][row_s:row_s + 64,
                                         128 + i * 64:128 + (i + 1) * 64]
                                .bitcast(BF16)).then_inc(p.sS.h, 1)
                        p.sS.n += 1
                    s_.wait_ge(p.vS.h, v_base + VP * khc + VP)
                    s_.copy(Sb[:, h * 1024:(h + 1) * 1024],
                            Spl[:, h * 1024:(h + 1) * 1024]).then_inc(p.sS.h, 1)
                    p.sS.n += 1

        @blk.vector
        def _(v):
            for c in range(4):
                for h in range(4):
                    khc = 4 * c + h
                    vb = v_base + VP * khc
                    sb = sbase(c, h)
                    v.wait_ge(p.sS.h, sb + 4)
                    if khc >= 1:
                        v.wait_ge(p.tS.h, t_base + TP * (khc - 1) + 16)
                    for ph in range(2):
                        qs = qpl[:, (2 * h + ph) * TC + c * CH:
                                 (2 * h + ph) * TC + (c + 1) * CH]
                        ks = kpl[:, (2 * h + ph) * TC + c * CH:
                                 (2 * h + ph) * TC + (c + 1) * CH]
                        v.tensor_mul(qgb[:, ph * CH:(ph + 1) * CH], qs,
                                     e1[:, ph * CH:(ph + 1) * CH])
                        v.tensor_mul(kdf[:, ph * CH:(ph + 1) * CH], ks,
                                     e2[:, ph * CH:(ph + 1) * CH])
                        v.tensor_copy(kdb[:, ph * CH:(ph + 1) * CH],
                                      kdf[:, ph * CH:(ph + 1) * CH])
                        v.tensor_scalar(kgb[:, ph * CH:(ph + 1) * CH],
                                        kdf[:, ph * CH:(ph + 1) * CH],
                                        e4c[:, 2 * h + ph:2 * h + ph + 1], None,
                                        ALU.mult)
                    v.engine_nop().then_inc(p.vS.h, 1)
                    p.vS.n += 1
                    row = (c % 2) * 64
                    v.wait_ge(p.tS.h, t_base + TP * khc + 2)
                    v.tensor_mul(atb[row:row + 64, 0:64],
                                 p.psA[1][row:row + 64, 0:64],
                                 p.maskut_t[row:row + 64, 0:64]).then_inc(p.vS.h, 1)
                    p.vS.n += 1
                    v.wait_ge(p.tS.h, t_base + TP * khc + TP)
                    for ph in range(2):
                        ssl = slice(h * 1024 + ph * 512, h * 1024 + (ph + 1) * 512)
                        v.tensor_scalar(Spl[:, ssl], Spl[:, ssl],
                                        e4c[:, 2 * h + ph:2 * h + ph + 1], None,
                                        ALU.mult).then_inc(p.vS.h, 1)
                        v.tensor_add(Spl[:, ssl], Spl[:, ssl],
                                     p.psB[:, ph * 512:(ph + 1) * 512])\
                            .then_inc(p.vS.h, 2)
                        p.vS.n += 3

        @blk.tensor
        def _(t):
            for c in range(4):
                for h in range(4):
                    khc = 4 * c + h
                    vb = v_base + VP * khc
                    sb = sbase(c, h)
                    tt, row = c // 2, (c % 2) * 64
                    t.wait_ge(p.vS.h, vb + 1)
                    if khc >= 1:
                        t.wait_ge(p.sS.h, sbase(c, h - 1) + 10 if h > 0
                                  else sbase(c - 1, 3) + 10)
                    for ph in range(2):
                        t.matmul(p.psA[1][row:row + 64, 0:64],
                                 kdb[:, ph * CH:(ph + 1) * CH],
                                 qgb[:, ph * CH:(ph + 1) * CH],
                                 start=(ph == 0), stop=(ph == 1),
                                 tile_position=(0, row)).then_inc(p.tS.h, 1)
                        p.tS.n += 1
                    t.wait_ge(p.vS.h, vb + 2)
                    if khc >= 1:
                        t.wait_ge(p.sS.h, (sbase(c, h - 1) if h > 0
                                           else sbase(c - 1, 3)) + 8)
                    for dvc in range(4):
                        t.matmul(p.psA[0][:, dvc * CH:(dvc + 1) * CH],
                                 vpl[row:row + 64,
                                     tt * 2048 + h * 512 + dvc * 128:
                                     tt * 2048 + h * 512 + (dvc + 1) * 128],
                                 atb[row:row + 64, 0:64], start=True, stop=False)\
                            .then_inc(p.tS.h, 1)
                        p.tS.n += 1
                        for ph in range(2):
                            t.matmul(p.psA[0][:, dvc * CH:(dvc + 1) * CH],
                                     Sb[:, h * 1024 + ph * 512 + dvc * 128:
                                        h * 1024 + ph * 512 + (dvc + 1) * 128],
                                     qgb[:, ph * CH:(ph + 1) * CH],
                                     start=False, stop=(ph == 1)).then_inc(p.tS.h, 1)
                            p.tS.n += 1
                    t.wait_ge(p.vS.h, vb + 2)
                    for i in range(2):
                        t.transpose(p.psA[1][row:row + 64,
                                             128 + i * 64:128 + (i + 1) * 64]
                                    .bitcast(BF16),
                                    kgb[:, i * CH:(i + 1) * CH], p.identb_t[:],
                                    tile_position=(0, row)).then_inc(p.tS.h, 1)
                        p.tS.n += 1
                    t.wait_ge(p.sS.h, sb + 10)
                    if khc >= 1:
                        t.wait_ge(p.vS.h, vb - VP + VP)
                    for ph in range(2):
                        t.matmul(p.psB[:, ph * 512:(ph + 1) * 512],
                                 kgtb[row:row + 64, ph * 128:(ph + 1) * 128],
                                 vpl[row:row + 64, tt * 2048 + h * 512:
                                     tt * 2048 + (h + 1) * 512],
                                 start=True, stop=True).then_inc(p.tS.h, 1)
                        p.tS.n += 1


def gla_final(p, tc, xn3, opl, gsc, aol, wsl, qpl, kpl):
    nc = p.nc
    TC = 256

    def xmov(kc):
        return xn3[:, kc * T + tc * TC: kc * T + tc * TC + TC]

    with nc.Block() as blk:
        t_base = p.tS.n
        s_base = p.sS.n
        qw_base = [p.qW[i].n for i in range(4)]

        @blk.sync
        def _(sync):
            for m in range(16):
                for kc in range(8):
                    j = m * 8 + kc
                    slot = j % 4
                    if j >= 4:
                        sync.wait_ge(p.tS.h, t_base + (j - 3))
                    sync.dma_start(wsl[:, slot * 128:(slot + 1) * 128],
                                   p.wscr["L1_attn_Wg"][kc * 128:(kc + 1) * 128,
                                                        m * 128:(m + 1) * 128])\
                        .then_inc(p.qW[slot].h, 16)
                    p.qW[slot].n += 16

        @blk.tensor
        def _(t):
            for m in range(16):
                if m >= 2:
                    t.wait_ge(p.sS.h, s_base + (m - 1))
                for kc in range(8):
                    j = m * 8 + kc
                    slot = j % 4
                    t.wait_ge(p.qW[slot].h, qw_base[slot] + 16 * (j // 4 + 1))
                    t.matmul(p.psA[m % 2][:, 0:TC],
                             wsl[:, slot * 128:(slot + 1) * 128], xmov(kc),
                             start=(kc == 0), stop=(kc == 7)).then_inc(p.tS.h, 1)
                    p.tS.n += 1

        @blk.scalar
        def _(s_):
            for m in range(16):
                s_.wait_ge(p.tS.h, t_base + 8 * (m + 1))
                s_.activation(gsc[:, m * TC:(m + 1) * TC], p.psA[m % 2][:, 0:TC],
                              AF.Silu).then_inc(p.sS.h, 1)
                p.sS.n += 1

    with nc.Block() as blk:
        t_base = p.tS.n
        s_base = p.sS.n
        v_base = p.vS.n
        rwc = p.cols["L1_attn_rms_w"]

        @blk.scalar
        def _(s_):
            for h in range(4):
                for dvc in range(4):
                    i = 4 * h + dvc
                    if i >= 2:
                        s_.wait_ge(p.tS.h, t_base + 5 * ((i - 2) // 4) + (i - 2) % 4 + 1)
                    sq = p.sqtile if dvc % 2 == 0 else p.vtmp
                    s_.activation(sq[:, 0:TC], opl[:, i * TC:(i + 1) * TC],
                                  AF.Square).then_inc(p.sS.h, 1)
                    p.sS.n += 1
                s_.wait_ge(p.vS.h, v_base + 7 * h + 1)
                s_.activation(p.row_b[0:1, 0:TC], p.row_a[0:1, 0:TC], AF.Sqrt)\
                    .then_inc(p.sS.h, 1)
                p.sS.n += 1

        @blk.tensor
        def _(t):
            for h in range(4):
                for dvc in range(4):
                    t.wait_ge(p.sS.h, s_base + 5 * h + dvc + 1)
                    if dvc == 0 and h >= 1:
                        t.wait_ge(p.vS.h, v_base + 7 * (h - 1) + 1)
                    sq = p.sqtile if dvc % 2 == 0 else p.vtmp
                    t.matmul(p.psC[0:1, 0:TC], p.ones_col[:, 0:1], sq[:, 0:TC],
                             start=(dvc == 0), stop=(dvc == 3)).then_inc(p.tS.h, 1)
                    p.tS.n += 1
                t.wait_ge(p.vS.h, v_base + 7 * h + 2)
                t.matmul(p.psD[:, 0:TC], p.ones_row[0:1, :], p.row_rs[0:1, 0:TC],
                         start=True, stop=True).then_inc(p.tS.h, 1)
                p.tS.n += 1

        @blk.vector
        def _(v):
            for h in range(4):
                v.wait_ge(p.tS.h, t_base + 5 * h + 4)
                v.tensor_scalar(p.row_a[0:1, 0:TC], p.psC[0:1, 0:TC],
                                FC(1.0 / 512.0), FC(EPS), ALU.mult, ALU.add)\
                    .then_inc(p.vS.h, 1)
                p.vS.n += 1
                v.wait_ge(p.sS.h, s_base + 5 * (h + 1))
                v.reciprocal(p.row_rs[0:1, 0:TC], p.row_b[0:1, 0:TC])\
                    .then_inc(p.vS.h, 1)
                p.vS.n += 1
                v.wait_ge(p.tS.h, t_base + 5 * h + 5)
                v.engine_nop().then_inc(p.vS.h, 1)
                p.vS.n += 1
                for dvc in range(4):
                    i = 4 * h + dvc
                    v.scalar_tensor_tensor(p.psB[:, 0:TC], opl[:, i * TC:(i + 1) * TC],
                                           rwc[:, dvc:dvc + 1], p.psD[:, 0:TC],
                                           ALU.mult, ALU.mult)
                    dst = qpl if i < 8 else kpl
                    v.tensor_mul(dst[:, (i % 8) * TC:(i % 8 + 1) * TC],
                                 p.psB[:, 0:TC], gsc[:, i * TC:(i + 1) * TC])\
                        .then_inc(p.vS.h, 1)
                    p.vS.n += 1

    with nc.Block() as blk:
        t_base = p.tS.n
        s_base = p.sS.n
        qw_base = [p.qW[i].n for i in range(4)]

        @blk.sync
        def _(sync):
            for m in range(8):
                for kc in range(16):
                    j = m * 16 + kc
                    slot = j % 4
                    if j >= 4:
                        sync.wait_ge(p.tS.h, t_base + (j - 3))
                    sync.dma_start(wsl[:, slot * 128:(slot + 1) * 128],
                                   p.wscr["L1_attn_Wo"][kc * 128:(kc + 1) * 128,
                                                        m * 128:(m + 1) * 128])\
                        .then_inc(p.qW[slot].h, 16)
                    p.qW[slot].n += 16

        @blk.tensor
        def _(t):
            for m in range(8):
                if m >= 2:
                    t.wait_ge(p.sS.h, s_base + (m - 1))
                for kc in range(16):
                    j = m * 16 + kc
                    slot = j % 4
                    t.wait_ge(p.qW[slot].h, qw_base[slot] + 16 * (j // 4 + 1))
                    srcpl = qpl if kc < 8 else kpl
                    t.matmul(p.psA[m % 2][:, 0:TC],
                             wsl[:, slot * 128:(slot + 1) * 128],
                             srcpl[:, (kc % 8) * TC:(kc % 8 + 1) * TC],
                             start=(kc == 0), stop=(kc == 15)).then_inc(p.tS.h, 1)
                    p.tS.n += 1

        @blk.scalar
        def _(s_):
            for m in range(8):
                s_.wait_ge(p.tS.h, t_base + 16 * (m + 1))
                s_.copy(aol[:, m * TC:(m + 1) * TC], p.psA[m % 2][:, 0:TC])\
                    .then_inc(p.sS.h, 1)
                p.sS.n += 1


def build_l1_ffn(p):
    nc = p.nc
    with contextlib.ExitStack() as es:
        xn4 = es.enter_context(nc.sbuf_tensor("f1_xn", [128, 8 * T], BF16))
        hq = es.enter_context(nc.sbuf_tensor("f1_hq", [128, 24 * TCW], BF16))
        sgt = es.enter_context(nc.sbuf_tensor("f1_sg", [128, 2 * TCW], F32))
        rg = es.enter_context(nc.sbuf_tensor("f1_rg", [128, 2 * 1024], BF16))
        ru = es.enter_context(nc.sbuf_tensor("f1_ru", [128, 2 * 1024], BF16))
        rd = es.enter_context(nc.sbuf_tensor("f1_rd", [128, 4 * 1024], BF16))
        op = es.enter_context(nc.sbuf_tensor("f1_op", [128, 8 * TCW], F32))

        plain_preln_plane(p, xn4)

        def psout(m):
            if m < 4:
                return p.psB[:, m * 512:m * 512 + TCW]
            return [p.psA[0], p.psA[1], p.psC, p.psD][m - 4][:, 0:TCW]

        for tc in range(T // TCW):
            with nc.Block() as blk:
                t_base = p.tS.n
                s_base = p.sS.n
                v_base = p.vS.n
                qx0 = [p.qX[i].n for i in range(2)]
                qz0 = [p.qZ[i].n for i in range(2)]
                qw0 = [p.qW[i].n for i in range(4)]

                @blk.sync
                def _(sync):
                    for s in range(24):
                        if s >= 2:
                            sync.wait_ge(p.tS.h, t_base + 16 * (s - 2) + 8)
                        sync.dma_start(rg[:, (s % 2) * 1024:(s % 2) * 1024 + 1024],
                                       p.wg_t[s * 128:(s + 1) * 128, :])\
                            .then_inc(p.qX[s % 2].h, 16)
                        p.qX[s % 2].n += 16
                        if s >= 2:
                            sync.wait_ge(p.tS.h, t_base + 16 * (s - 2) + 16)
                        sync.dma_start(ru[:, (s % 2) * 1024:(s % 2) * 1024 + 1024],
                                       p.wu_t[s * 128:(s + 1) * 128, :])\
                            .then_inc(p.qZ[s % 2].h, 16)
                        p.qZ[s % 2].n += 16
                    for s in range(24):
                        if s >= 4:
                            sync.wait_ge(p.tS.h, t_base + 384 + 8 * (s - 3))
                        sync.dma_start(rd[:, (s % 4) * 1024:(s % 4) * 1024 + 1024],
                                       p.wd_t[s * 128:(s + 1) * 128, :])\
                            .then_inc(p.qW[s % 4].h, 16)
                        p.qW[s % 4].n += 16

                @blk.tensor
                def _(t):
                    for s in range(24):
                        t.wait_ge(p.qX[s % 2].h, qx0[s % 2] + 16 * (s // 2 + 1))
                        if s >= 1:
                            t.wait_ge(p.sS.h, s_base + s)
                        for kc in range(8):
                            t.matmul(p.psA[0][:, 0:TCW],
                                     rg[:, (s % 2) * 1024 + kc * 128:
                                        (s % 2) * 1024 + (kc + 1) * 128],
                                     xn4[:, kc * T + tc * TCW: kc * T + tc * TCW + TCW],
                                     start=(kc == 0), stop=(kc == 7)).then_inc(p.tS.h, 1)
                            p.tS.n += 1
                        t.wait_ge(p.qZ[s % 2].h, qz0[s % 2] + 16 * (s // 2 + 1))
                        if s >= 1:
                            t.wait_ge(p.vS.h, v_base + s)
                        for kc in range(8):
                            t.matmul(p.psA[1][:, 0:TCW],
                                     ru[:, (s % 2) * 1024 + kc * 128:
                                        (s % 2) * 1024 + (kc + 1) * 128],
                                     xn4[:, kc * T + tc * TCW: kc * T + tc * TCW + TCW],
                                     start=(kc == 0), stop=(kc == 7)).then_inc(p.tS.h, 1)
                            p.tS.n += 1
                    t.wait_ge(p.sS.h, s_base + 24)
                    t.wait_ge(p.vS.h, v_base + 24)
                    for s in range(24):
                        t.wait_ge(p.qW[s % 4].h, qw0[s % 4] + 16 * (s // 4 + 1))
                        for m in range(8):
                            t.matmul(psout(m),
                                     rd[:, (s % 4) * 1024 + m * 128:
                                        (s % 4) * 1024 + (m + 1) * 128],
                                     hq[:, s * TCW:(s + 1) * TCW],
                                     start=(s == 0), stop=(s == 23)).then_inc(p.tS.h, 1)
                            p.tS.n += 1

                @blk.scalar
                def _(s_):
                    for s in range(24):
                        s_.wait_ge(p.tS.h, t_base + 16 * s + 8)
                        if s >= 2:
                            s_.wait_ge(p.vS.h, v_base + s - 1)
                        s_.activation(sgt[:, (s % 2) * TCW:(s % 2) * TCW + TCW],
                                      p.psA[0][:, 0:TCW], AF.Silu,
                                      scale=FC(1.0 / 127.0)).then_inc(p.sS.h, 1)
                        p.sS.n += 1
                    s_.wait_ge(p.tS.h, t_base + 576)
                    for m in range(8):
                        s_.activation(op[:, m * TCW:(m + 1) * TCW], psout(m),
                                      AF.Copy, scale=FC(1.0 / 16129.0)).then_inc(p.sS.h, 1)
                        p.sS.n += 1

                @blk.vector
                def _(v):
                    for s in range(24):
                        v.wait_ge(p.sS.h, s_base + s + 1)
                        v.wait_ge(p.tS.h, t_base + 16 * s + 16)
                        v.tensor_mul(hq[:, s * TCW:(s + 1) * TCW],
                                     sgt[:, (s % 2) * TCW:(s % 2) * TCW + TCW],
                                     p.psA[1][:, 0:TCW]).then_inc(p.vS.h, 1)
                        p.vS.n += 1

            post_ln_residual_tc(p, tc, lambda dc: op[:, dc * TCW:(dc + 1) * TCW])


def build_nc():
    nc = bass.Bass("TRN2", target_bir_lowering=False)
    p = P()
    p.nc = nc
    es = contextlib.ExitStack()
    p.es = es

    p.xin = nc.dram_tensor("x", [T, D], F32, kind="ExternalInput").ap()
    p.yout = nc.dram_tensor("y", [T, D], F32, kind="ExternalOutput").ap()
    p.win = {}
    wspecs = {
        "L0_attn_ln_g": [D], "L0_attn_ln_b": [D],
        "L0_attn_f_in_w": [2 * DI, D], "L0_attn_f_conv_w": [DI, 4], "L0_attn_f_conv_b": [DI],
        "L0_attn_f_xproj_w": [96, DI], "L0_attn_f_dt_w": [DI, DTRANK], "L0_attn_f_dt_b": [DI],
        "L0_attn_f_A_log": [DI, NSTATE], "L0_attn_f_D_skip": [DI], "L0_attn_f_out_w": [D, DI],
        "L0_attn_b_in_w": [2 * DI, D], "L0_attn_b_conv_w": [DI, 4], "L0_attn_b_conv_b": [DI],
        "L0_attn_b_xproj_w": [96, DI], "L0_attn_b_dt_w": [DI, DTRANK], "L0_attn_b_dt_b": [DI],
        "L0_attn_b_A_log": [DI, NSTATE], "L0_attn_b_D_skip": [DI], "L0_attn_b_out_w": [D, DI],
        "L0_ffn_w1": [D, 4096], "L0_ffn_b1": [4096], "L0_ffn_w2": [4096, D], "L0_ffn_b2": [D],
        "L1_attn_Wq": [D, 1024], "L1_attn_Wk": [D, 1024], "L1_attn_Wv": [D, 2048],
        "L1_attn_Wg": [D, 2048], "L1_attn_Wgk1": [D, 16], "L1_attn_Wgk2": [16, 1024],
        "L1_attn_bgk2": [1024], "L1_attn_Wo": [2048, D], "L1_attn_rms_w": [512],
    }
    for k, shp in wspecs.items():
        p.win[k] = nc.dram_tensor(k, shp, F32, kind="ExternalInput").ap()
    for k, shp in {"L1_ffn_w_gate": [3072, D], "L1_ffn_w_up": [3072, D],
                   "L1_ffn_w_down": [D, 3072]}.items():
        p.win[k] = nc.dram_tensor(k, shp, I32, kind="ExternalInput").ap()
    p.cin = {}
    for k, v in _consts_inputs().items():
        dt = F32 if v.dtype == np.float32 else BF16
        p.cin[k] = nc.dram_tensor(k, list(v.shape), dt, kind="ExternalInput").ap()

    sb = nc.alloc_sbuf_tensor
    p.r = sb("r", [128, 8 * T], F32)
    p.identf_t = sb("identf_t", [128, 128], F32)
    p.identb_t = sb("identb_t", [128, 128], BF16)
    p.maskut_t = sb("maskut_t", [128, 64], F32)
    p.ones_col = sb("ones_col", [128, 1], F32)
    p.ones_row = sb("ones_row", [1, 128], F32)
    p.row_m = sb("row_m", [1, 512], F32)
    p.row_rs = sb("row_rs", [1, 512], F32)
    p.row_b = sb("row_b", [1, 512], F32)
    p.sqtile = sb("sqtile", [128, 512], F32)
    p.vtmp = sb("vtmp", [128, 512], F32)
    p.zeros = sb("zeros", [128, 512], F32)
    _pv = _float_pool_values()
    p.cpool = sb("cpool", [128, len(_pv)], F32)
    for i, v in enumerate(_pv):
        nc.const_aps.register(F32, v, p.cpool[:, i:i + 1])

    pt = nc.alloc_psum_tensor
    p.psA = [pt("psA0", [128, 512], F32), pt("psA1", [128, 512], F32)]
    p.psB = pt("psB", [128, 2048], F32)
    p.psC = pt("psC", [128, 512], F32)
    p.psD = pt("psD", [128, 512], F32)
    p.pst = [p.psA[0], p.psA[1]]
    p.psrow = [p.psC, p.psD]
    p.row_v = p.psC[96:97, :]
    p.row_a = p.psD[96:97, :]
    p.psbc = [p.psA[0], p.psA[1]]

    p.qS = Ctr(es, nc, "qS")
    p.qG = Ctr(es, nc, "qG")
    p.qH = Ctr(es, nc, "qH")
    p.qW = [Ctr(es, nc, "qW%d" % i) for i in range(4)]
    p.qX = [Ctr(es, nc, "qX%d" % i) for i in range(2)]
    p.qZ = [Ctr(es, nc, "qZ%d" % i) for i in range(2)]
    p.qY = [Ctr(es, nc, "qY%d" % i) for i in range(2)]
    p.tS = Ctr(es, nc, "tS")
    p.sS = Ctr(es, nc, "sS")
    p.vS = Ctr(es, nc, "vS")
    p.gS = Ctr(es, nc, "gS")

    with nc.Block() as blk:
        @blk.sync
        def _(sync):
            for name, dst in [("c_identf", p.identf_t), ("c_identb", p.identb_t),
                              ("c_maskut", p.maskut_t), ("c_pool", p.cpool)]:
                sync.dma_start(dst[:], p.cin[name]).then_inc(p.qS.h, 16)
                p.qS.n += 16
            sync.wait_ge(p.qS.h, p.qS.n)

        @blk.vector
        def _(v):
            v.memset(p.ones_col[:], 1.0)
            v.memset(p.ones_row[:], 1.0)
            v.memset(p.zeros[:], 0.0)

    prep_weights(p)
    input_transpose(p)
    load_cols(p)

    if EN_L0A:
        build_l0_attn(p)
    if EN_L0F:
        build_l0_ffn(p)
    if EN_L1A:
        build_l1_attn(p)
    if EN_L1F:
        build_l1_ffn(p)

    output_transpose(p)
    es.close()
    return nc


_consts_cache = _consts_inputs()


def kernel(**inputs):
    x = np.ascontiguousarray(np.asarray(inputs["L0_attn_x"], dtype=np.float32))
    nc = build_nc()
    in_maps = []
    for b in range(8):
        m = {"x": x[b]}
        for k, v in inputs.items():
            if k == "L0_attn_x":
                continue
            m[k] = np.ascontiguousarray(np.asarray(v))
        for k, v in _consts_cache.items():
            m[k] = v
        in_maps.append(m)
    res = run_bass_kernel_spmd(nc, in_maps, core_ids=list(range(8)))
    return np.stack([np.asarray(r["y"], dtype=np.float32) for r in res.results])
